# version 11
# baseline (speedup 1.0000x reference)
import numpy as np
import concourse.bass as bass
import concourse.mybir as mybir
from concourse.bass_utils import run_bass_kernel_spmd

F32 = mybir.dt.float32
BF16 = mybir.dt.bfloat16
AF = mybir.ActivationFunctionType
ALU = mybir.AluOpType

NCORES = 8
T = 2048
TAU = 512
NT = T // TAU
WIN = 256
D = 1024
NL = 6
HID = 2816
NHC = HID // 128
Z = T + 1
SCR_COLS = 33400

ENGINES = ("tensor", "vector", "scalar", "sync", "gpsimd")


class Sched:

    def __init__(self):
        self.q = {e: [] for e in ENGINES}
        self.cnt = {}
        self.lastwait = {}
        self.sems = None

    ENG_SEM = {"vector": "v", "scalar": "s"}

    def op(self, eng, fn, inc=None):
        es = self.ENG_SEM.get(eng)
        if es is not None:
            self.wait(eng, self.tok(es))
        tok = None
        if inc is not None:
            sem, amt = inc
            self.cnt[sem] = self.cnt.get(sem, 0) + amt
            tok = (sem, self.cnt[sem])

            def wrapped(e, fn=fn, sem=sem, amt=amt):
                fn(e).then_inc(self.sems[sem], amt)
        else:
            def wrapped(e, fn=fn):
                fn(e)
        self.q[eng].append(wrapped)
        return tok

    def wait(self, eng, tok):
        if tok is None:
            return
        sem, val = tok
        if val <= 0:
            return
        key = (eng, sem)
        if self.lastwait.get(key, -1) >= val:
            return
        self.lastwait[key] = val

        def w(e, sem=sem, val=val):
            e.wait_ge(self.sems[sem], val)
        self.q[eng].append(w)

    def tok(self, sem):
        return (sem, self.cnt.get(sem, 0))

    def barrier(self):
        toks = {"tensor": self.tok("t"), "vector": self.tok("v"),
                "scalar": self.tok("s")}
        for e in ENGINES:
            for oe, tk in toks.items():
                if oe != e:
                    self.wait(e, tk)


class Banks:
    def __init__(self, S):
        self.S = S
        self.free = [None] * 8

    def acquire(self, i):
        self.S.wait("tensor", self.free[i])

    def release(self, i, tok):
        self.free[i] = tok


def _tables():
    half = np.arange(0, 64, 2, dtype=np.float64)
    inv_freq = 1.0 / 10000.0 ** (half / 64.0)
    t = np.arange(T, dtype=np.float64)
    fr = t[:, None] * inv_freq[None, :]
    base = (half + 0.4 * 64.0) / (1.4 * 64.0)
    power = (t - T // 2) / (WIN // 2)
    sc = base[None, :] ** power[:, None]
    cos, sin = np.cos(fr), np.sin(fr)

    def dup(a):
        b = np.concatenate([a.T, a.T], axis=0)
        return np.concatenate([b, b], axis=0)
    out = np.stack([dup(cos * sc), dup(sin * sc), dup(cos / sc), dup(sin / sc)])
    return np.ascontiguousarray(out.astype(np.float32))


def _consts():
    j = np.arange(128)[:, None]
    i = np.arange(128)[None, :]
    tge = (j >= i).astype(np.float32)
    tle = (j <= i).astype(np.float32)
    R = np.zeros((128, 128), np.float32)
    for blk in range(2):
        for m in range(64):
            pi = m + 32 if m < 32 else m - 32
            R[blk * 64 + pi, blk * 64 + m] = -1.0 if m < 32 else 1.0
    bd = np.zeros((128, 128), np.float32)
    bd[:64, :64] = 1.0
    bd[64:, 64:] = 1.0
    mean = np.full((128, 128), 1.0 / 1024.0, np.float32)
    ones = np.ones((128, 128), np.float32)
    return np.ascontiguousarray(np.stack([tge, tle, R, bd, mean, ones]))


def _prep(inp):
    f = np.float32
    pr = {}
    x = np.asarray(inp["L0_attn_x"], f)
    xz = np.zeros((NCORES, D, Z), f)
    for c in range(NCORES):
        xz[c, :, 1:] = x[c].T
    pr["x0z"] = xz

    W_in = np.asarray(inp["L0_attn_W_in"], f)
    pr["w_int"] = np.ascontiguousarray(
        W_in.T.reshape(8, 128, 2, 1024).transpose(2, 0, 1, 3))
    pr["w_st"] = np.ascontiguousarray(np.asarray(inp["L0_attn_W_s"], f).T.reshape(8, 128, 1024))
    pr["w_ot"] = np.ascontiguousarray(np.asarray(inp["L0_attn_W_o"], f).T.reshape(8, 128, 1024))
    mpar = np.zeros((128, 32), f)
    mpar[:, 0:16] = np.asarray(inp["L0_attn_b_in"], f).reshape(16, 128).T
    mpar[:, 16:24] = np.asarray(inp["L0_attn_b_s"], f).reshape(8, 128).T
    mpar[:, 24:32] = np.asarray(inp["L0_attn_b_o"], f).reshape(8, 128).T
    pr["mpar"] = mpar

    Wqkv = np.asarray(inp["L0_ffn_Wqkv"], f)
    qk = np.empty((NL, 8, 8, 128, 256), f)
    for p in range(8):
        qk[:, p, :, :, 0:128] = Wqkv[:, :, p * 128:(p + 1) * 128].reshape(NL, 8, 128, 128)
        qk[:, p, :, :, 128:256] = Wqkv[:, :, 1024 + p * 128:1024 + (p + 1) * 128].reshape(NL, 8, 128, 128)
    pr["wqkvqk"] = np.ascontiguousarray(qk)
    pr["wqkvv"] = np.ascontiguousarray(Wqkv[:, :, 2048:].reshape(NL, 8, 128, 1024))

    Wo = np.asarray(inp["L0_ffn_Wo"], f)
    pr["wo"] = np.ascontiguousarray(
        Wo.reshape(NL, 8, 128, 8, 128).transpose(0, 3, 1, 2, 4))

    W1 = np.asarray(inp["L0_ffn_W1"], f)
    a = np.zeros((NL, D, HID), f)
    g = np.zeros((NL, D, HID), f)
    a[:, :, :2730] = W1[:, :, :2730]
    g[:, :, :2730] = W1[:, :, 2730:]
    w1r = np.empty((NL, NHC, 8, 128, 256), f)
    w1r[..., 0:128] = a.reshape(NL, 8, 128, NHC, 128).transpose(0, 3, 1, 2, 4)
    w1r[..., 128:256] = g.reshape(NL, 8, 128, NHC, 128).transpose(0, 3, 1, 2, 4)
    pr["w1r"] = np.ascontiguousarray(w1r)

    W2 = np.asarray(inp["L0_ffn_W2"], f)
    w2p = np.zeros((NL, HID, D), f)
    w2p[:, :2730, :] = W2
    pr["w2p"] = np.ascontiguousarray(w2p.reshape(NL, NHC, 128, D))

    lpar = np.zeros((NL, 128, 70), f)
    lpar[:, :, 0:8] = np.asarray(inp["L0_ffn_ln_g"], f).reshape(NL, 8, 128).transpose(0, 2, 1)
    lpar[:, :, 8:16] = np.asarray(inp["L0_ffn_ln_b"], f).reshape(NL, 8, 128).transpose(0, 2, 1)
    lpar[:, :, 16:24] = np.asarray(inp["L0_ffn_b2"], f).reshape(NL, 8, 128).transpose(0, 2, 1)
    qs = np.asarray(inp["L0_ffn_q_scale"], f)
    ks = np.asarray(inp["L0_ffn_k_scale"], f)
    lpar[:, :, 24] = np.concatenate([qs, qs], axis=1)
    lpar[:, :, 25] = np.concatenate([ks, ks], axis=1)
    b1 = np.asarray(inp["L0_ffn_b1"], f)
    b1a = np.zeros((NL, HID), f)
    b1g = np.zeros((NL, HID), f)
    b1a[:, :2730] = b1[:, :2730]
    b1g[:, :2730] = b1[:, 2730:]
    lpar[:, :, 26:48] = b1a.reshape(NL, NHC, 128).transpose(0, 2, 1)
    lpar[:, :, 48:70] = b1g.reshape(NL, NHC, 128).transpose(0, 2, 1)
    pr["lpar"] = lpar

    pr["tab"] = _tables()
    pr["con"] = _consts()
    return pr


N_LAYERS = NL
DBG = False
TRUNC = False
TRUNC_AT = None


def build_nc():
    nc = bass.Bass("TRN2", target_bir_lowering=False, detect_race_conditions=not DBG)
    H = {}
    H["x0z"] = nc.dram_tensor("x0z", [D, Z], F32, kind="ExternalInput").ap()
    H["w_int"] = nc.dram_tensor("w_int", [2, 8, 128, 1024], F32, kind="ExternalInput").ap()
    H["w_st"] = nc.dram_tensor("w_st", [8, 128, 1024], F32, kind="ExternalInput").ap()
    H["w_ot"] = nc.dram_tensor("w_ot", [8, 128, 1024], F32, kind="ExternalInput").ap()
    H["mpar"] = nc.dram_tensor("mpar", [128, 32], F32, kind="ExternalInput").ap()
    H["wqkvqk"] = nc.dram_tensor("wqkvqk", [NL, 8, 8, 128, 256], F32, kind="ExternalInput").ap()
    H["wqkvv"] = nc.dram_tensor("wqkvv", [NL, 8, 128, 1024], F32, kind="ExternalInput").ap()
    H["wo"] = nc.dram_tensor("wo", [NL, 8, 8, 128, 128], F32, kind="ExternalInput").ap()
    H["w1r"] = nc.dram_tensor("w1r", [NL, NHC, 8, 128, 256], F32, kind="ExternalInput").ap()
    H["w2p"] = nc.dram_tensor("w2p", [NL, NHC, 128, D], F32, kind="ExternalInput").ap()
    H["lpar"] = nc.dram_tensor("lpar", [NL, 128, 70], F32, kind="ExternalInput").ap()
    H["tab"] = nc.dram_tensor("tab", [4, 128, T], F32, kind="ExternalInput").ap()
    H["con"] = nc.dram_tensor("con", [6, 128, 128], F32, kind="ExternalInput").ap()
    H["x1scr"] = nc.dram_tensor("x1scr", [D, T], F32, kind="Internal").ap()
    H["s_w1"] = nc.dram_tensor("s_w1", [NL, NHC, 8, 128, 512], BF16, kind="Internal").ap()
    H["s_w2"] = nc.dram_tensor("s_w2", [NL, NHC, 128, 2048], BF16, kind="Internal").ap()
    if DBG:
        H["dbg"] = nc.dram_tensor("dbg", [128, 16384], F32, kind="ExternalOutput").ap()
    H["out"] = nc.dram_tensor("out", [D, T], F32, kind="ExternalOutput").ap()

    import contextlib
    stack = contextlib.ExitStack()

    def alloc(name, cols, dt=F32):
        return stack.enter_context(nc.sbuf_tensor(name, [128, cols], dt))

    Y = alloc("Y", 8 * Z)
    CON = alloc("CON", 6 * 128)
    TAB = alloc("TAB", 4 * TAU)
    LPAR = alloc("LPAR", 70)
    MPAR = alloc("MPAR", 32)
    EPSA = alloc("EPSA", 1)
    SCR = alloc("SCR", SCR_COLS)
    ps = [stack.enter_context(nc.psum_tensor(f"ps{i}", [128, 512], F32)) for i in range(8)]

    S = Sched()
    B = Banks(S)

    def yt(d):
        return Y[:, d * Z:(d + 1) * Z]

    def ysl(d, t0, n, shifted):
        off = t0 if (shifted and d >= 4) else t0 + 1
        return Y[:, d * Z + off: d * Z + off + n]

    def con(i):
        return CON[:, i * 128:(i + 1) * 128]
    TGE, TLE, RPERM, BD64, MEAN, ONES = (con(i) for i in range(6))

    class Area:
        def __init__(self, off=0):
            self.off = off

        def take(self, cols):
            a = SCR[:, self.off:self.off + cols]
            self.off += cols
            assert self.off <= SCR_COLS, f"SCR overflow {self.off}"
            return a

        def takeb(self, bcols):
            cols = (bcols + 1) // 2
            a = SCR[:, self.off:self.off + cols].bitcast(BF16)
            self.off += cols
            assert self.off <= SCR_COLS, f"SCR overflow {self.off}"
            return a

    def phase0():
        for d_ in range(8):
            S.op("gpsimd", lambda e, d=d_: e.dma_start(
                out=yt(d), in_=H["x0z"][d * 128:(d + 1) * 128, :]), inc=("g", 16))
        S.op("gpsimd", lambda e: e.dma_start(
            out=CON[:, :].rearrange("p (k n) -> p k n", k=6),
            in_=H["con"].rearrange("k p n -> p k n")), inc=("g", 16))
        S.op("gpsimd", lambda e: e.dma_start(out=MPAR[:, :], in_=H["mpar"]), inc=("g", 16))
        ep = S.op("vector", lambda e: e.memset(EPSA[:, 0:1], 1e-5), inc=("v", 1))
        S.wait("scalar", ep)
        g0 = S.tok("g")
        for e in ("tensor", "vector", "scalar", "sync"):
            S.wait(e, g0)

    class LNCtx:

        def __init__(self, ar):
            self.sqs = [ar.take(512), ar.take(512)]
            self.MS = ar.take(512)
            self.VAR = ar.take(512)
            self.RS = ar.take(512)
            self.MU = ar.take(512)
            self.sq_rd = [None, None]
            self.ms_rd = None
            self.mu_rd = None
            self.rs_rd = None

    def ln_compute(ctx, t0, shifted, mu_bank, sq_bank):
        B.acquire(mu_bank)
        B.acquire(sq_bank)
        mtok = stok = None
        sqtoks = []
        for d_ in range(8):
            sl = ysl(d_, t0, TAU, shifted)
            buf = ctx.sqs[d_ % 2]
            if d_ < 2:
                S.wait("scalar", ctx.sq_rd[d_ % 2])
                sqtoks.append(S.op("scalar", lambda e, sl=sl, b=buf: e.activation(
                    out=b, in_=sl, func=AF.Square), inc=("s", 1)))
            mtok = S.op("tensor", lambda e, sl=sl, st=(d_ == 0), sp=(d_ == 7): e.matmul(
                ps[mu_bank][:, :], MEAN, sl, start=st, stop=sp), inc=("t", 1))
        for d_ in range(8):
            sl = ysl(d_, t0, TAU, shifted)
            buf = ctx.sqs[d_ % 2]
            if d_ >= 2:
                S.wait("scalar", ctx.sq_rd[d_ % 2])
                sqtoks.append(S.op("scalar", lambda e, sl=sl, b=buf: e.activation(
                    out=b, in_=sl, func=AF.Square), inc=("s", 1)))
            S.wait("tensor", sqtoks[d_])
            stok = S.op("tensor", lambda e, b=buf, st=(d_ == 0), sp=(d_ == 7): e.matmul(
                ps[sq_bank][:, :], MEAN, b, start=st, stop=sp), inc=("t", 1))
            ctx.sq_rd[d_ % 2] = stok
        S.wait("scalar", mtok)
        S.wait("scalar", ctx.ms_rd)
        t1 = S.op("scalar", lambda e: e.activation(
            out=ctx.MS, in_=ps[mu_bank][:, :], func=AF.Square), inc=("s", 1))
        S.wait("scalar", ctx.mu_rd)
        tmu = S.op("scalar", lambda e: e.activation(
            out=ctx.MU, in_=ps[mu_bank][:, :], func=AF.Identity), inc=("s", 1))
        S.wait("vector", stok)
        S.wait("vector", t1)
        S.wait("vector", ctx.rs_rd)
        t2 = S.op("vector", lambda e: e.scalar_tensor_tensor(
            out=ctx.VAR, in0=ps[sq_bank][:, :], scalar=1.0, in1=ctx.MS,
            op0=ALU.mult, op1=ALU.subtract), inc=("v", 1))
        ctx.ms_rd = t2
        B.release(mu_bank, tmu)
        B.release(sq_bank, t2)
        S.wait("scalar", t2)
        tln = S.op("scalar", lambda e: e.activation(
            out=ctx.VAR, in_=ctx.VAR, func=AF.Ln, bias=EPSA[:, 0:1]), inc=("s", 1))
        S.wait("scalar", tln)
        trs = S.op("scalar", lambda e: e.activation(
            out=ctx.RS, in_=ctx.VAR, func=AF.Exp, scale=-0.5), inc=("s", 1))
        return tmu, trs

    PROL_BASE = 31352
    prol_done = [None] * NL
    prol_state = {"tin_rd": [None, None], "thi_rd": [None, None], "tlo_rd": [None, None],
                  "cast": [None, None], "sub": [None, None]}

    def emit_split_unit(ar_off, ui, src_ap, hi_out, lo_out, ncols):
        st = prol_state
        slot = ui % 2
        TIN = SCR[:, ar_off + slot * 512: ar_off + slot * 512 + ncols]
        THI = SCR[:, ar_off + 1024 + slot * 256: ar_off + 1024 + slot * 256 + (ncols + 1) // 2].bitcast(BF16)
        TLO = SCR[:, ar_off + 1536 + slot * 256: ar_off + 1536 + slot * 256 + (ncols + 1) // 2].bitcast(BF16)
        itok = S.op("gpsimd", lambda e, a=src_ap, t=TIN: e.dma_start(out=t, in_=a),
                    inc=(f"pi{slot}", 16))
        st["tin_rd"][slot] = itok
        S.wait("gpsimd", itok)
        S.wait("gpsimd", st["thi_rd"][slot])
        ct = S.op("gpsimd", lambda e, t=TIN, h=THI: e.tensor_copy(
            out=h, in_=t), inc=("pg", 1))
        st["cast"][slot] = ct
        S.wait("gpsimd", ct)
        S.wait("gpsimd", st["tlo_rd"][slot])
        sb = S.op("gpsimd", lambda e, t=TIN, h=THI, l=TLO: e.tensor_tensor(
            out=l, in0=t, in1=h, op=ALU.subtract), inc=("pg", 1))
        st["sub"][slot] = sb
        S.wait("gpsimd", sb)
        ht = S.op("gpsimd", lambda e, h=THI, o=hi_out: e.dma_start(out=o, in_=h),
                  inc=(f"po{slot}", 16))
        st["thi_rd"][slot] = ht
        lt = S.op("gpsimd", lambda e, l=TLO, o=lo_out: e.dma_start(out=o, in_=l),
                  inc=(f"po{2 + slot}", 16))
        st["tlo_rd"][slot] = lt
        return lt

    def emit_split_ffn(li, ar_off, quarter=None):
        ui = [0]
        hcs = range(NHC) if quarter is None else range(
            quarter * 6, min(NHC, (quarter + 1) * 6))
        for hc in hcs:
            w1v = H["w1r"][li, hc].rearrange("d p n -> p d n")
            s1v = H["s_w1"][li, hc].rearrange("d p n -> p d n")
            for dh in range(2):
                dsl = slice(dh * 4, dh * 4 + 4)
                emit_split_unit(ar_off, ui[0], w1v[:, dsl, 0:128],
                                s1v[:, dsl, 0:128], s1v[:, dsl, 256:384], 512)
                ui[0] += 1
                emit_split_unit(ar_off, ui[0], w1v[:, dsl, 128:256],
                                s1v[:, dsl, 128:256], s1v[:, dsl, 384:512], 512)
                ui[0] += 1
            for wh in range(2):
                emit_split_unit(ar_off, ui[0], H["w2p"][li, hc][:, wh * 512:(wh + 1) * 512],
                                H["s_w2"][li, hc][:, wh * 512:wh * 512 + 512],
                                H["s_w2"][li, hc][:, 1024 + wh * 512:1024 + wh * 512 + 512], 512)
                ui[0] += 1
        if quarter is None or quarter == 3:
            prol_done[li] = [S.tok(f"po{i}") for i in range(4)] + [S.tok("pg")]

    def mamba():
        ar = Area()
        LNB = [ar.take(512) for _ in range(8)]
        GT = [ar.take(512) for _ in range(8)]
        VL = [ar.take(512) for _ in range(8)]
        AB = [ar.take(513) for _ in range(8)]
        BB = [ar.take(513) for _ in range(8)]
        TMP = [ar.take(512), ar.take(512)]
        ring = [ar.take(1024) for _ in range(4)]
        lctx = LNCtx(ar)
        ring_free = [None] * 4
        ring_rdy = [None] * 4
        rslot = [0]

        for e8 in range(8):
            S.op("vector", lambda e, a=AB[e8]: e.memset(a[:, 0:1], 0.0), inc=("v", 1))

        def load_slab(hbm_ap):
            i = rslot[0] % 4
            rslot[0] += 1
            S.wait("sync", ring_free[i])
            tok = S.op("sync", lambda e, h=hbm_ap, sl=ring[i]: e.dma_start(
                out=sl[:, :], in_=h), inc=(f"d{i}", 16))
            ring_rdy[i] = tok
            return i

        lnb_w = [None] * 8
        lnb_rd = [None] * 8
        gt_w = [None] * 8
        gt_rd = [None] * 8
        vl_w = [None] * 8
        vl_rd = [None] * 8
        bb_rd = [None] * 8
        tmp_rd = [None, None]
        scan_toks = [None] * 8

        for tau in range(NT):
            t0 = tau * TAU
            if N_LAYERS > 0:
                emit_split_ffn(0, PROL_BASE, quarter=tau)
            tmu, trs = ln_compute(lctx, t0, False, 0, 1)
            mu_cons = rs_cons = None
            for d_ in range(8):
                sl = ysl(d_, t0, TAU, False)
                S.wait("vector", lnb_rd[d_])
                S.wait("vector", tmu)
                S.op("vector", lambda e, sl=sl, d=d_: e.scalar_tensor_tensor(
                    out=LNB[d], in0=sl, scalar=1.0, in1=lctx.MU,
                    op0=ALU.mult, op1=ALU.subtract), inc=("v", 1))
                S.wait("vector", trs)
                lt = S.op("vector", lambda e, d=d_: e.tensor_tensor(
                    out=LNB[d], in0=LNB[d], in1=lctx.RS, op=ALU.mult), inc=("v", 1))
                lnb_w[d_] = lt
                mu_cons = rs_cons = lt
            lctx.mu_rd = mu_cons
            lctx.rs_rd = rs_cons

            for eg in range(2):
                banktok = [None] * 8
                for d_ in range(8):
                    si = load_slab(H["w_int"][eg, d_])
                    S.wait("tensor", ring_rdy[si])
                    S.wait("tensor", lnb_w[d_])
                    tk = None
                    for e8 in range(8):
                        if d_ == 0:
                            B.acquire(e8)
                        tk = S.op("tensor", lambda e, si=si, e8=e8, d=d_: e.matmul(
                            ps[e8][:, :], ring[si][:, e8 * 128:(e8 + 1) * 128], LNB[d],
                            start=(d == 0), stop=(d == 7)), inc=("t", 1))
                        banktok[e8] = tk
                    ring_free[si] = tk
                    lnb_rd[d_] = tk
                for e8 in range(8):
                    S.wait("scalar", banktok[e8])
                    if eg == 0:
                        S.wait("scalar", gt_rd[e8])
                        ev = S.op("scalar", lambda e, e8=e8: e.activation(
                            out=GT[e8], in_=ps[e8][:, :], func=AF.Sigmoid,
                            bias=MPAR[:, e8:e8 + 1]), inc=("s", 1))
                        gt_w[e8] = ev
                    else:
                        S.wait("scalar", vl_rd[e8])
                        ev = S.op("scalar", lambda e, e8=e8: e.activation(
                            out=VL[e8], in_=ps[e8][:, :], func=AF.Identity,
                            bias=MPAR[:, 8 + e8:8 + e8 + 1]), inc=("s", 1))
                        vl_w[e8] = ev
                    B.release(e8, ev)

            banktok = [None] * 8
            for d_ in range(8):
                si = load_slab(H["w_st"][d_])
                S.wait("tensor", ring_rdy[si])
                S.wait("tensor", vl_w[d_])
                tk = None
                for e8 in range(8):
                    if d_ == 0:
                        B.acquire(e8)
                    tk = S.op("tensor", lambda e, si=si, e8=e8, d=d_: e.matmul(
                        ps[e8][:, :], ring[si][:, e8 * 128:(e8 + 1) * 128], VL[d],
                        start=(d == 0), stop=(d == 7)), inc=("t", 1))
                    banktok[e8] = tk
                ring_free[si] = tk
                vl_rd[d_] = tk
            for e8 in range(8):
                tm = TMP[e8 % 2]
                S.wait("scalar", banktok[e8])
                S.wait("scalar", tmp_rd[e8 % 2])
                ev = S.op("scalar", lambda e, e8=e8, tm=tm: e.activation(
                    out=tm, in_=ps[e8][:, :], func=AF.Tanh,
                    bias=MPAR[:, 16 + e8:16 + e8 + 1]), inc=("s", 1))
                B.release(e8, ev)
                S.wait("vector", bb_rd[e8])
                if tau > 0:
                    S.op("vector", lambda e, e8=e8: e.tensor_copy(
                        out=BB[e8][:, 0:1], in_=BB[e8][:, 512:513]), inc=("v", 1))
                else:
                    S.op("vector", lambda e, e8=e8: e.memset(BB[e8][:, 0:1], 0.0), inc=("v", 1))
                S.wait("vector", ev)
                S.wait("vector", gt_w[e8])
                btk = S.op("vector", lambda e, e8=e8, tm=tm: e.tensor_tensor(
                    out=BB[e8][:, 1:513], in0=tm, in1=GT[e8], op=ALU.mult), inc=("v", 1))
                tmp_rd[e8 % 2] = btk
                at = S.op("vector", lambda e, e8=e8: e.tensor_scalar(
                    out=AB[e8][:, 1:513], in0=GT[e8], scalar1=-1.0, scalar2=1.0,
                    op0=ALU.mult, op1=ALU.add), inc=("v", 1))
                gt_rd[e8] = at
                sc = S.op("vector", lambda e, e8=e8: e.tensor_tensor_scan(
                    out=BB[e8][:, :], data0=AB[e8][:, :], data1=BB[e8][:, :],
                    initial=0.0, op0=ALU.mult, op1=ALU.add), inc=("v", 1))
                scan_toks[e8] = sc

            for d_ in range(8):
                si = load_slab(H["w_ot"][d_])
                S.wait("tensor", ring_rdy[si])
                S.wait("tensor", scan_toks[d_])
                tk = None
                for oc in range(8):
                    if d_ == 0:
                        B.acquire(oc)
                    tk = S.op("tensor", lambda e, si=si, oc=oc, d=d_: e.matmul(
                        ps[oc][:, :], ring[si][:, oc * 128:(oc + 1) * 128], BB[d][:, 1:513],
                        start=(d == 0), stop=(d == 7)), inc=("t", 1))
                ring_free[si] = tk
                bb_rd[d_] = tk
            wo_done = S.tok("t")
            for oc in range(8):
                sl = ysl(oc, t0, TAU, False)
                S.wait("vector", wo_done)
                u1 = S.op("vector", lambda e, oc=oc, sl=sl: e.scalar_tensor_tensor(
                    out=sl, in0=ps[oc][:, :], scalar=MPAR[:, 24 + oc:24 + oc + 1],
                    in1=sl, op0=ALU.add, op1=ALU.add), inc=("v", 1))
                B.release(oc, u1)
                u2 = S.op("vector", lambda e, oc=oc, sl=sl: e.tensor_tensor(
                    out=sl, in0=sl, in1=LNB[oc], op=ALU.add), inc=("v", 1))
                lnb_rd[oc] = u2

        S.barrier()
        for d_ in range(8):
            S.op("gpsimd", lambda e, d=d_: e.dma_start(
                out=H["x1scr"][d * 128:(d + 1) * 128, :], in_=yt(d)[:, 1:]), inc=("gx", 16))
        gsave = S.tok("gx")
        for tau in range(NT):
            t0 = tau * TAU
            tmu, trs = ln_compute(lctx, t0, False, 0, 1)
            S.wait("vector", gsave)
            last = None
            for d_ in range(8):
                sl = ysl(d_, t0, TAU, False)
                S.wait("vector", tmu)
                S.op("vector", lambda e, sl=sl: e.scalar_tensor_tensor(
                    out=sl, in0=sl, scalar=1.0, in1=lctx.MU,
                    op0=ALU.mult, op1=ALU.subtract), inc=("v", 1))
                S.wait("vector", trs)
                last = S.op("vector", lambda e, sl=sl: e.tensor_tensor(
                    out=sl, in0=sl, in1=lctx.RS, op=ALU.mult), inc=("v", 1))
            lctx.mu_rd = last
            lctx.rs_rd = last
        S.barrier()

    def attn_layer(li):
        ar = Area()
        QB = ar.take(4096)
        KR = ar.take(8 * 3 * 256)
        VT = [ar.take(1040) for _ in range(6)]
        OSB = ar.take(4096)
        pt_base = ar.off
        PT = [ar.take(256) for _ in range(8)]
        SU = ar.take(512)
        RB = ar.take(512)
        OTT = [ar.take(256), ar.take(256)]
        SQ2 = ar.take(512)
        SS = ar.take(512)
        Q0 = ar.take(512)
        T1 = ar.take(512)
        DL = [[ar.take(1) for _ in range(8)] for _ in range(2)]
        wring = [ar.take(2048), ar.take(2048)]
        vring = [ar.take(1024)]
        ar.take(PROL_BASE - ar.off)
        oring = [ar.take(1024), ar.take(1024)]
        lctx_unused = None
        wring_free = [None, None]
        wring_rdy = [None, None]
        vring_free = [None, None]
        vring_rdy = [None, None]
        oring_free = [None, None]
        oring_rdy = [None, None]

        ltok = S.op("gpsimd", lambda e, li=li: e.dma_start(out=LPAR[:, :], in_=H["lpar"][li]), inc=("gl", 16))
        S.wait("vector", ltok)
        S.wait("scalar", ltok)
        QS = LPAR[:, 24:25]
        KS = LPAR[:, 25:26]

        kr_wr = [None] * 3
        kr_rd = [None] * 3
        vt_wr = [None] * 6
        vt_rd = [None] * 6
        pt_wr = [None] * 8
        pt_rd = [None] * 8
        qpsets = [
            {"SQ2": SQ2, "SS": SS, "Q0": Q0, "T1": T1, "ptb": False,
             "sq2_rd": None, "q0_rd": None, "ss_srd": None, "guard": []},
            {"SQ2": SCR[:, pt_base:pt_base + 512],
             "SS": SCR[:, pt_base + 512:pt_base + 1024],
             "Q0": SCR[:, pt_base + 1024:pt_base + 1536],
             "T1": SCR[:, pt_base + 1536:pt_base + 2048], "ptb": True,
             "sq2_rd": None, "q0_rd": None, "ss_srd": None, "guard": []},
        ]
        ptb_guard = []
        su_rd = None
        rb_rd = None
        ott_rd = [None, None]
        qb_rd = None
        osb_rd = None
        tab_rd = None

        for idx, half in ((0, 1), (3, 0), (4, 1), (7, 0)):
            tk = S.op("vector", lambda e, i=idx, h=half: e.memset(
                PT[i][:, h * 128:(h + 1) * 128], 0.0), inc=("v", 1))
            pt_wr[idx] = tk

        def q_proc(bankQ, mmtok, is_q, dests, ts):
            SQ2_, SS_, Q0_, T1_ = ts["SQ2"], ts["SS"], ts["Q0"], ts["T1"]
            sc_tab = QS if is_q else KS
            A_sl = TAB[:, 0:TAU] if is_q else TAB[:, 2 * TAU:3 * TAU]
            S_sl = TAB[:, TAU:2 * TAU] if is_q else TAB[:, 3 * TAU:4 * TAU]
            S.wait("scalar", mmtok)
            S.wait("scalar", ts["sq2_rd"])
            if ts["ptb"]:
                for i in (0, 1):
                    S.wait("scalar", pt_wr[i])
                    S.wait("scalar", pt_rd[i])
            t1 = S.op("scalar", lambda e, b=bankQ, o=SQ2_: e.activation(
                out=o, in_=ps[b][:, :], func=AF.Square), inc=("s", 1))
            S.wait("tensor", t1)
            B.acquire(4)
            t2 = S.op("tensor", lambda e, i_=SQ2_: e.matmul(
                ps[4][:, :], BD64, i_, start=True, stop=True), inc=("t", 1))
            ts["sq2_rd"] = t2
            S.wait("vector", t2)
            S.wait("vector", ts["ss_srd"])
            if ts["ptb"]:
                for i in (2, 3):
                    S.wait("vector", pt_wr[i])
                    S.wait("vector", pt_rd[i])
            t3 = S.op("vector", lambda e, o=SS_: e.tensor_scalar(
                out=o, in0=ps[4][:, :], scalar1=1e-18, scalar2=None, op0=ALU.max), inc=("v", 1))
            B.release(4, t3)
            S.wait("scalar", t3)
            tl = S.op("scalar", lambda e, o=SS_: e.activation(out=o, in_=o, func=AF.Ln), inc=("s", 1))
            S.wait("scalar", tl)
            t5 = S.op("scalar", lambda e, o=SS_: e.activation(
                out=o, in_=o, func=AF.Exp, scale=-0.5), inc=("s", 1))
            ts["ss_srd"] = t5
            S.wait("vector", t5)
            S.wait("vector", ts["q0_rd"])
            if ts["ptb"]:
                for i in (4, 5):
                    S.wait("vector", pt_wr[i])
                    S.wait("vector", pt_rd[i])
            t6 = S.op("vector", lambda e, b=bankQ, o=Q0_, i_=SS_: e.tensor_tensor(
                out=o, in0=ps[b][:, :], in1=i_, op=ALU.mult), inc=("v", 1))
            B.release(bankQ, t6)
            S.op("vector", lambda e, sc=sc_tab, o=Q0_: e.tensor_scalar(
                out=o, in0=o, scalar1=sc, scalar2=None, op0=ALU.mult), inc=("v", 1))
            t7 = S.tok("v")
            S.wait("tensor", t7)
            B.acquire(5)
            t8 = S.op("tensor", lambda e, i_=Q0_: e.matmul(
                ps[5][:, :], RPERM, i_, start=True, stop=True), inc=("t", 1))
            ts["q0_rd"] = t8
            if ts["ptb"]:
                for i in (6, 7):
                    S.wait("vector", pt_wr[i])
                    S.wait("vector", pt_rd[i])
            S.op("vector", lambda e, A=A_sl, o=T1_, i_=Q0_: e.tensor_tensor(
                out=o, in0=i_, in1=A, op=ALU.mult), inc=("v", 1))
            S.wait("vector", t8)
            t10 = S.op("vector", lambda e, Ss=S_sl, o=Q0_: e.tensor_tensor(
                out=o, in0=ps[5][:, :], in1=Ss, op=ALU.mult), inc=("v", 1))
            B.release(5, t10)
            toks = []
            off = 0
            for (ap_, n_) in dests:
                tk = S.op("vector", lambda e, a=ap_, o=off, n=n_, t1_=T1_, q0_=Q0_: e.tensor_tensor(
                    out=a, in0=t1_[:, o:o + n], in1=q0_[:, o:o + n], op=ALU.add), inc=("v", 1))
                toks.append(tk)
                off += n_
            ts["guard"] = [t2, t8, toks[-1]]
            return toks

        for tau in range(NT):
            t0 = tau * TAU
            S.wait("sync", tab_rd)
            ttok = None
            for k in range(4):
                ttok = S.op("sync", lambda e, k=k, t0=t0: e.dma_start(
                    out=TAB[:, k * TAU:(k + 1) * TAU], in_=H["tab"][k, :, t0:t0 + TAU]), inc=("dt", 16))
            S.wait("vector", ttok)

            S.wait("vector", qb_rd)
            w0, w1 = 2 * tau, 2 * tau + 1
            ktoks = [None, None]
            for p in range(8):
                wi = p % 2
                S.wait("sync", wring_free[wi])
                wtk = S.op("sync", lambda e, p=p, wi=wi, li=li: e.dma_start(
                    out=wring[wi][:, :].rearrange("p (d n) -> p d n", d=8),
                    in_=H["wqkvqk"][li, p].rearrange("d p n -> p d n")), inc=(f"d{4 + wi}", 16))
                wring_rdy[wi] = wtk
                S.wait("tensor", wtk)
                bq, bkk = (0, 1) if p % 2 == 0 else (2, 3)
                B.acquire(bq)
                B.acquire(bkk)
                tk = None
                qtk = None
                for d_ in range(8):
                    sl = ysl(d_, t0, TAU, True)
                    qtk = S.op("tensor", lambda e, wi=wi, d=d_, sl=sl, bq=bq: e.matmul(
                        ps[bq][:, :], wring[wi][:, d * 256:d * 256 + 128], sl,
                        start=(d == 0), stop=(d == 7)), inc=("t", 1))
                    tk = S.op("tensor", lambda e, wi=wi, d=d_, sl=sl, bkk=bkk: e.matmul(
                        ps[bkk][:, :], wring[wi][:, d * 256 + 128:(d + 1) * 256], sl,
                        start=(d == 0), stop=(d == 7)), inc=("t", 1))
                wring_free[wi] = tk
                if DBG and li == 0 and tau == 0 and p == 0:
                    S.wait("scalar", qtk)
                    S.wait("scalar", tk)
                    dq = S.op("scalar", lambda e: e.activation(out=T1[:, :], in_=ps[0][:, :], func=AF.Identity), inc=("s", 1))
                    S.wait("gpsimd", dq)
                    S.wait("gpsimd", S.tok("gd"))
                    dqd = S.op("gpsimd", lambda e: e.dma_start(out=H["dbg"][:, 0:512], in_=T1[:, :]), inc=("gd", 16))
                    S.wait("scalar", dqd)
                    dk = S.op("scalar", lambda e: e.activation(out=T1[:, :], in_=ps[1][:, :], func=AF.Identity), inc=("s", 1))
                    S.wait("gpsimd", dk)
                    S.wait("gpsimd", S.tok("gd"))
                    dkd = S.op("gpsimd", lambda e: e.dma_start(out=H["dbg"][:, 512:1024], in_=T1[:, :]), inc=("gd", 16))
                    S.wait("gpsimd", S.tok("gd"))
                    S.op("gpsimd", lambda e: e.dma_start(out=H["dbg"][:, 1024:1536], in_=Y[:, 0:512]), inc=("gd", 16))
                    S.wait("gpsimd", S.tok("gd"))
                    S.op("gpsimd", lambda e: e.dma_start(out=H["dbg"][:, 1536:2048], in_=Y[:, 4 * Z:4 * Z + 512]), inc=("gd", 16))
                    S.wait("gpsimd", S.tok("gd"))
                    S.op("gpsimd", lambda e: e.dma_start(out=H["dbg"][:, 2048:4096], in_=wring[wi][:, :]), inc=("gd", 16))
                    S.wait("vector", dkd)
                qset = qpsets[p % 2]
                q_proc(bq, qtk, True, [(QB[:, p * 512:(p + 1) * 512], 512)], qset)
                s0 = KR[:, p * 768 + (w0 % 3) * 256: p * 768 + (w0 % 3) * 256 + 256]
                s1 = KR[:, p * 768 + (w1 % 3) * 256: p * 768 + (w1 % 3) * 256 + 256]
                S.wait("vector", kr_rd[w0 % 3])
                S.wait("vector", kr_rd[w1 % 3])
                kt = q_proc(bkk, tk, False, [(s0, 256), (s1, 256)], qset)
                ktoks = kt
                if TRUNC:
                    return

            kr_wr[w0 % 3] = ktoks[0]
            kr_wr[w1 % 3] = ktoks[1]
            ptb_guard = qpsets[1]["guard"]
            for i, h in ((0, 1), (3, 0), (4, 1), (7, 0)):
                for gt in ptb_guard:
                    S.wait("vector", gt)
                S.wait("vector", pt_rd[i])
                mtk = S.op("vector", lambda e, i=i, h=h: e.memset(
                    PT[i][:, h * 128:(h + 1) * 128], 0.0), inc=("v", 1))
                pt_wr[i] = mtk
            tab_rd = S.tok("v")
            if TRUNC_AT == (li, tau, "a2"):
                return
            if DBG and li == 0 and tau == 0:
                S.wait("gpsimd", S.tok("v"))
                S.wait("gpsimd", S.tok("gd"))
                S.op("gpsimd", lambda e: e.dma_start(
                    out=H["dbg"][:, 5632:6144], in_=KR[:, 0:512]), inc=("gd", 16))
                S.wait("gpsimd", S.tok("gd"))
                S.op("gpsimd", lambda e: e.dma_start(
                    out=H["dbg"][:, 6144:6656], in_=QB[:, 0:512]), inc=("gd", 16))

            for sci in range(4):
                slot = (tau * 4 + sci) % 6
                S.wait("vector", vt_rd[slot])
                S.op("vector", lambda e, s=slot: e.memset(
                    VT[s][:, :].rearrange("p (h n) -> p h n", n=65)[:, :, 64:65], 1.0), inc=("v", 1))
            vmem = S.tok("v")
            banktok = [None] * 8
            for d_ in range(8):
                for hh in range(2):
                    S.wait("sync", vring_free[hh])
                    vtk = S.op("sync", lambda e, d=d_, hh=hh, li=li: e.dma_start(
                        out=vring[0][:, hh * 512:(hh + 1) * 512],
                        in_=H["wqkvv"][li, d][:, hh * 512:(hh + 1) * 512]),
                        inc=(f"d{6 if hh == 0 else 9}", 16))
                    vring_rdy[hh] = vtk
                    S.wait("tensor", vtk)
                    tk = None
                    for sci in range(4):
                        tsub0 = t0 + sci * 128
                        sl = ysl(d_, tsub0, 128, True)
                        bk = sci * 2 + hh
                        if d_ == 0:
                            B.acquire(bk)
                        tk = S.op("tensor", lambda e, sl=sl, hh=hh, bk=bk, d=d_: e.matmul(
                            ps[bk][:, :], sl, vring[0][:, hh * 512:(hh + 1) * 512],
                            start=(d == 0), stop=(d == 7)), inc=("t", 1))
                        banktok[bk] = tk
                    vring_free[hh] = tk
            for sci in range(4):
                slot = (tau * 4 + sci) % 6
                ev = None
                for hh in range(2):
                    bk = sci * 2 + hh
                    S.wait("scalar", banktok[bk])
                    S.wait("scalar", vmem)
                    S.wait("scalar", vt_rd[slot])
                    ev = S.op("scalar", lambda e, s=slot, hh=hh, bk=bk: e.activation(
                        out=VT[s][:, :].rearrange("p (h n) -> p h n", n=65)[:, hh * 8:(hh + 1) * 8, 0:64],
                        in_=ps[bk][:, :].rearrange("p (h n) -> p h n", n=64),
                        func=AF.Identity), inc=("s", 1))
                    B.release(bk, ev)
                vt_wr[slot] = ev
            if TRUNC_AT == (li, tau, "a3"):
                return
            qkv_done = S.tok("t")

            if tau > 0:
                for oc in range(8):
                    S.wait("vector", qkv_done)
                    S.op("vector", lambda e, oc=oc, tp=t0 - 1, tau=tau: e.tensor_copy(
                        out=Y[:, oc * Z + 1 + tp: oc * Z + 2 + tp],
                        in_=DL[(tau - 1) % 2][oc]), inc=("v", 1))

            a4v = S.tok("v")
            S.wait("tensor", a4v)
            pending = None
            osbA_tok = [[None] * 8, [None] * 8]
            osbB_tok = [[None] * 8, [None] * 8]
            if True:
                def qk_both(p, w, qcols):
                    if w == 0:
                        jcs = [(0, 0, "oc0", 1), (0, 1, "oc1", 3)]
                    else:
                        jcs = [(w - 1, 0, "jc0", 0), (w - 1, 1, "jc1", 1),
                               (w, 0, "jc2", 2), (w, 1, "jc3", 3)]
                    ptiles = [[], []]
                    for idx, (kw, kcs, role, tno) in enumerate(jcs):
                        for hb in (0, 1):
                            qk_role(p, hb, hb * 64, w, qcols, idx, kw, kcs,
                                    role, tno, ptiles[hb])
                    return ptiles

                def qk_role(p, hb, hbase, w, qcols, idx, kw, kcs, role, tno, ptile):
                    if True:
                        pt_i = hb * 4 + tno
                        krsl = KR[hbase:hbase + 64,
                                  p * 768 + (kw % 3) * 256 + kcs * 128:
                                  p * 768 + (kw % 3) * 256 + kcs * 128 + 128]
                        bk = hb * 2 + (idx % 2)
                        B.acquire(bk)
                        S.wait("tensor", kr_wr[kw % 3])
                        if role == "jc0":
                            rhs = QB[hbase:hbase + 64, p * 512 + qcols: p * 512 + qcols + 128]
                            mm = S.op("tensor", lambda e, k=krsl, r=rhs, bk=bk: e.matmul(
                                ps[bk][:, 0:128], k, r, start=True, stop=True), inc=("t", 1))
                            kr_rd[kw % 3] = mm
                            S.wait("scalar", mm)
                            S.wait("scalar", pt_rd[pt_i])
                            for gt_ in ptb_guard:
                                S.wait("scalar", gt_)
                            ev = S.op("scalar", lambda e, bk=bk, pi=pt_i: e.activation(
                                out=PT[pi][:, 0:128], in_=ps[bk][:, 0:128],
                                func=AF.Exp, scale=8.0), inc=("s", 1))
                            S.wait("vector", ev)
                            mk = S.op("vector", lambda e, pi=pt_i: e.tensor_tensor(
                                out=PT[pi][:, 0:128], in0=PT[pi][:, 0:128],
                                in1=TGE, op=ALU.mult), inc=("v", 1))
                            pt_wr[pt_i] = mk
                        elif role in ("jc3", "oc1"):
                            rhs = QB[hbase:hbase + 64, p * 512 + qcols + 128: p * 512 + qcols + 256]
                            mm = S.op("tensor", lambda e, k=krsl, r=rhs, bk=bk: e.matmul(
                                ps[bk][:, 0:128], k, r, start=True, stop=True), inc=("t", 1))
                            kr_rd[kw % 3] = mm
                            S.wait("scalar", mm)
                            S.wait("scalar", pt_rd[pt_i])
                            for gt_ in ptb_guard:
                                S.wait("scalar", gt_)
                            ev = S.op("scalar", lambda e, bk=bk, pi=pt_i: e.activation(
                                out=PT[pi][:, 128:256], in_=ps[bk][:, 0:128],
                                func=AF.Exp, scale=8.0), inc=("s", 1))
                            S.wait("vector", ev)
                            mk = S.op("vector", lambda e, pi=pt_i: e.tensor_tensor(
                                out=PT[pi][:, 128:256], in0=PT[pi][:, 128:256],
                                in1=TLE, op=ALU.mult), inc=("v", 1))
                            pt_wr[pt_i] = mk
                        else:
                            rhs = QB[hbase:hbase + 64, p * 512 + qcols: p * 512 + qcols + 256]
                            mm = S.op("tensor", lambda e, k=krsl, r=rhs, bk=bk: e.matmul(
                                ps[bk][:, 0:256], k, r, start=True, stop=True), inc=("t", 1))
                            kr_rd[kw % 3] = mm
                            S.wait("scalar", mm)
                            S.wait("scalar", pt_rd[pt_i])
                            for gt_ in ptb_guard:
                                S.wait("scalar", gt_)
                            ev = S.op("scalar", lambda e, bk=bk, pi=pt_i: e.activation(
                                out=PT[pi][:, 0:256], in_=ps[bk][:, 0:256],
                                func=AF.Exp, scale=8.0), inc=("s", 1))
                            pt_wr[pt_i] = ev
                            S.wait("vector", ev)
                            if role == "jc1":
                                mk = S.op("vector", lambda e, pi=pt_i: e.tensor_tensor(
                                    out=PT[pi][:, 128:256], in0=PT[pi][:, 128:256],
                                    in1=TGE, op=ALU.mult), inc=("v", 1))
                            else:
                                mk = S.op("vector", lambda e, pi=pt_i: e.tensor_tensor(
                                    out=PT[pi][:, 0:128], in0=PT[pi][:, 0:128],
                                    in1=TLE, op=ALU.mult), inc=("v", 1))
                            pt_wr[pt_i] = mk
                        B.release(bk, pt_wr[pt_i])
                        ptile.append((pt_i, kw, kcs))

                def pv_accum(p, hb, head, ptile, ob):
                    nonlocal su_rd
                    B.acquire(ob)
                    nmm = len(ptile)
                    pv = None
                    for idx2, (pi, kw, kcs) in enumerate(ptile):
                        slot = (kw * 2 + kcs) % 6
                        vsl = VT[slot][:, head * 65: head * 65 + 65]
                        S.wait("tensor", pt_wr[pi])
                        S.wait("tensor", vt_wr[slot])
                        pv = S.op("tensor", lambda e, v=vsl, pi=pi, ob=ob,
                                  st=(idx2 == 0), sp=(idx2 == nmm - 1): e.matmul(
                            ps[ob][0:65, 0:256], v, PT[pi][:, :], start=st, stop=sp), inc=("t", 1))
                        pt_rd[pi] = pv
                        vt_rd[slot] = pv
                    S.wait("scalar", pv)
                    S.wait("scalar", su_rd)
                    S.op("scalar", lambda e, ob=ob, hb=hb: e.activation(
                        out=SU[64:65, hb * 256:hb * 256 + 256],
                        in_=ps[ob][64:65, 0:256], func=AF.Ln), inc=("s", 1))

                def pv_exp():
                    return S.op("scalar", lambda e: e.activation(
                        out=SU[64:65, :], in_=SU[64:65, :], func=AF.Exp, scale=-1.0), inc=("s", 1))

                def pv_finish(p, wl, qcols, pva, pvb, bb, l2):
                    nonlocal su_rd, rb_rd, osb_rd
                    B.acquire(bb)
                    S.wait("tensor", l2)
                    bc = S.op("tensor", lambda e, bb=bb: e.matmul(
                        ps[bb][0:64, 0:512], ONES[64:65, 0:64], SU[64:65, :],
                        start=True, stop=True), inc=("t", 1))
                    su_rd = bc
                    S.wait("scalar", bc)
                    S.wait("scalar", rb_rd)
                    rb = S.op("scalar", lambda e, bb=bb: e.activation(
                        out=RB[0:64, :], in_=ps[bb][0:64, 0:512], func=AF.Identity), inc=("s", 1))
                    B.release(bb, rb)
                    S.wait("vector", rb)
                    dst = OSB[0:64, p * 512 + qcols: p * 512 + qcols + 256]
                    S.wait("vector", osb_rd)
                    oa = S.op("vector", lambda e, d=dst, pva=pva: e.tensor_tensor(
                        out=d, in0=ps[pva][0:64, 0:256], in1=RB[0:64, 0:256],
                        op=ALU.mult), inc=("v", 1))
                    osbA_tok[wl][p] = oa
                    oi = p % 2
                    S.wait("vector", ott_rd[oi])
                    oe = S.op("vector", lambda e, oi=oi, pvb=pvb: e.tensor_tensor(
                        out=OTT[oi][0:64, :], in0=ps[pvb][0:64, 0:256],
                        in1=RB[0:64, 256:512], op=ALU.mult), inc=("v", 1))
                    rb_rd = oe
                    B.release(pva, oe)
                    B.release(pvb, oe)
                    S.wait("gpsimd", oe)
                    S.wait("gpsimd", osb_rd)
                    dtk = S.op("gpsimd", lambda e, oi=oi, p=p, q=qcols: e.dma_start(
                        out=OSB[64:128, p * 512 + q: p * 512 + q + 256],
                        in_=OTT[oi][0:64, :]), inc=(f"go{oi}", 16))
                    ott_rd[oi] = dtk
                    osbB_tok[wl][p] = dtk

            for wl in range(2):
                w = 2 * tau + wl
                qcols = wl * 256
                for p in range(8):
                    ptA, ptB = qk_both(p, w, qcols)
                    if pending is not None:
                        pv_finish(*pending)
                    pva, pvb = (4, 5) if p % 2 == 0 else (6, 7)
                    pv_accum(p, 0, 2 * p, ptA, pva)
                    pv_accum(p, 1, 2 * p + 1, ptB, pvb)
                    pending = (p, wl, qcols, pva, pvb,
                               0 if p % 2 == 0 else 2, pv_exp())
            pv_finish(*pending)
            pending = None
            wo_last = None
            if tau == 0 and prol_done[li] is not None:
                for tk_ in prol_done[li]:
                    S.wait("sync", tk_)
            for oc in range(8):
                oi = oc % 2
                S.wait("sync", oring_free[oi])
                otk = S.op("sync", lambda e, oc=oc, oi=oi, li=li: e.dma_start(
                    out=oring[oi][:, :].rearrange("p (c n) -> p c n", c=8),
                    in_=H["wo"][li, oc].rearrange("c p n -> p c n")), inc=(f"d{7 + oi}", 16))
                oring_rdy[oi] = otk
                S.wait("tensor", otk)
                B.acquire(oc)
                tk = None
                for ic in range(8):
                    if oc == 0:
                        for wl_ in range(2):
                            S.wait("tensor", osbA_tok[wl_][ic])
                            S.wait("tensor", osbB_tok[wl_][ic])
                    tk = S.op("tensor", lambda e, oi=oi, ic=ic, oc=oc, st=(ic == 0), sp=(ic == 7): e.matmul(
                        ps[oc][:, :], oring[oi][:, ic * 128:(ic + 1) * 128],
                        OSB[:, ic * 512:(ic + 1) * 512], start=st, stop=sp), inc=("t", 1))
                oring_free[oi] = tk
                wo_last = tk
                sl = ysl(oc, t0, TAU, False)
                S.wait("vector", tk)
                S.op("vector", lambda e, oc=oc, sl=sl: e.tensor_tensor(
                    out=sl[:, 0:511], in0=ps[oc][:, 0:511], in1=sl[:, 0:511],
                    op=ALU.add), inc=("v", 1))
                u2 = S.op("vector", lambda e, oc=oc, sl=sl, tau=tau: e.tensor_tensor(
                    out=DL[tau % 2][oc], in0=ps[oc][:, 511:512], in1=sl[:, 511:512],
                    op=ALU.add), inc=("v", 1))
                B.release(oc, u2)
            qb_rd = wo_last
            osb_rd = wo_last
            if TRUNC_AT == (li, tau, "wo"):
                return

        for oc in range(8):
            S.op("vector", lambda e, oc=oc: e.tensor_copy(
                out=Y[:, oc * Z + Z - 1: oc * Z + Z], in_=DL[(NT - 1) % 2][oc]), inc=("v", 1))
        S.barrier()

    def ffn_layer(li):
        ar = Area()
        XNF = [ar.take(512) for _ in range(8)]
        XNH = [ar.takeb(512) for _ in range(8)]
        XNL = [ar.takeb(512) for _ in range(8)]
        H2H = [ar.takeb(512) for _ in range(NHC)]
        H2L = [ar.takeb(512) for _ in range(NHC)]
        H2T = [ar.take(512), ar.take(512)]
        GSB = [ar.take(512), ar.take(512)]
        DLF = [[ar.take(1) for _ in range(8)] for _ in range(2)]
        w1ring = [ar.takeb(4096), ar.takeb(4096)]
        w2ring = [ar.takeb(2048), ar.takeb(2048)]
        lctx = LNCtx(ar)
        prol_off = PROL_BASE
        assert ar.off <= PROL_BASE
        w1_free = [None, None]
        w1_rdy = [None, None]
        w2_free = [None, None]
        w2_rdy = [None, None]
        xnh_rd = [None] * 8
        xnf_wr = [None] * 8
        h2_rd = [None] * NHC
        gsb_rd = [None, None]
        h2t_rd = [None, None]

        ltok = S.op("gpsimd", lambda e, li=li: e.dma_start(out=LPAR[:, :], in_=H["lpar"][li]), inc=("gl", 16))
        S.wait("vector", ltok)
        S.wait("scalar", ltok)
        if prol_done[li] is not None:
            for tk_ in prol_done[li]:
                S.wait("sync", tk_)

        for tau in range(NT):
            t0 = tau * TAU
            if li + 1 < N_LAYERS:
                emit_split_ffn(li + 1, prol_off, quarter=tau)
            tmu, trs = ln_compute(lctx, t0, True, 0, 1)
            last = None
            for d_ in range(8):
                sl = ysl(d_, t0, TAU, True)
                S.wait("vector", xnh_rd[d_])
                S.wait("vector", tmu)
                S.op("vector", lambda e, sl=sl, d=d_: e.scalar_tensor_tensor(
                    out=XNF[d], in0=sl, scalar=1.0, in1=lctx.MU,
                    op0=ALU.mult, op1=ALU.subtract), inc=("v", 1))
                S.wait("vector", trs)
                S.op("vector", lambda e, d=d_: e.scalar_tensor_tensor(
                    out=XNF[d], in0=XNF[d], scalar=LPAR[:, d:d + 1], in1=lctx.RS,
                    op0=ALU.mult, op1=ALU.mult), inc=("v", 1))
                ch = S.op("vector", lambda e, d=d_: e.tensor_scalar(
                    out=XNF[d], in0=XNF[d], scalar1=LPAR[:, 8 + d:8 + d + 1],
                    scalar2=None, op0=ALU.add), inc=("v", 1))
                S.wait("scalar", ch)
                S.wait("scalar", xnh_rd[d_])
                ct = S.op("scalar", lambda e, d=d_: e.activation(
                    out=XNH[d], in_=XNF[d], func=AF.Identity), inc=("s", 1))
                S.wait("vector", ct)
                sb = S.op("vector", lambda e, d=d_: e.tensor_tensor(
                    out=XNL[d], in0=XNF[d], in1=XNH[d], op=ALU.subtract), inc=("v", 1))
                xnf_wr[d_] = sb
                last = sb
            lctx.mu_rd = last
            lctx.rs_rd = last

            if tau > 0:
                for oc in range(8):
                    S.op("vector", lambda e, oc=oc, tp=t0 - 1, tau=tau: e.tensor_copy(
                        out=Y[:, oc * Z + 1 + tp: oc * Z + 2 + tp],
                        in_=DLF[(tau - 1) % 2][oc]), inc=("v", 1))

            for hc in range(NHC):
                wi = hc % 2
                S.wait("sync", w1_free[wi])
                wtk = S.op("sync", lambda e, hc=hc, wi=wi, li=li: e.dma_start(
                    out=w1ring[wi][:, :].rearrange("p (d n) -> p d n", d=8),
                    in_=H["s_w1"][li, hc].rearrange("d p n -> p d n")), inc=(f"d{4 + wi}", 16))
                w1_rdy[wi] = wtk
                S.wait("tensor", wtk)
                ba, bg = (hc % 2) * 2, (hc % 2) * 2 + 1
                B.acquire(ba)
                B.acquire(bg)
                tk = None
                atk = None
                slab = w1ring[wi][:, :].rearrange("p (d n) -> p d n", d=8)
                for d_ in range(8):
                    st = (d_ == 0)
                    if hc == 0:
                        S.wait("tensor", xnf_wr[d_])
                    for pi_, (wof, xx) in enumerate(((0, XNH[d_]), (0, XNL[d_]), (256, XNH[d_]))):
                        atk = S.op("tensor", lambda e, sl=slab, d=d_, w=wof, x=xx, ba=ba,
                                   st=(st and pi_ == 0), sp=(d_ == 7 and pi_ == 2): e.matmul(
                            ps[ba][:, :], sl[:, d, w:w + 128], x,
                            start=st, stop=sp), inc=("t", 1))
                        tk = S.op("tensor", lambda e, sl=slab, d=d_, w=wof + 128, x=xx, bg=bg,
                                  st=(st and pi_ == 0), sp=(d_ == 7 and pi_ == 2): e.matmul(
                            ps[bg][:, :], sl[:, d, w:w + 128], x,
                            start=st, stop=sp), inc=("t", 1))
                w1_free[wi] = tk
                for d_ in range(8):
                    xnh_rd[d_] = tk
                hp = hc % 2
                S.wait("scalar", atk)
                S.wait("scalar", tk)
                S.wait("scalar", h2t_rd[hp])
                ea = S.op("scalar", lambda e, ba=ba, hc=hc, hp=hp: e.activation(
                    out=H2T[hp], in_=ps[ba][:, :], func=AF.Identity,
                    bias=LPAR[:, 26 + hc:26 + hc + 1]), inc=("s", 1))
                B.release(ba, ea)
                S.wait("scalar", gsb_rd[hp])
                eg = S.op("scalar", lambda e, bg=bg, hc=hc, hp=hp: e.activation(
                    out=GSB[hp], in_=ps[bg][:, :], func=AF.Gelu,
                    bias=LPAR[:, 48 + hc:48 + hc + 1]), inc=("s", 1))
                B.release(bg, eg)
                S.wait("vector", eg)
                S.wait("vector", ea)
                h2m = S.op("vector", lambda e, hp=hp: e.tensor_tensor(
                    out=H2T[hp], in0=H2T[hp], in1=GSB[hp], op=ALU.mult), inc=("v", 1))
                gsb_rd[hp] = h2m
                S.wait("scalar", h2m)
                S.wait("scalar", h2_rd[hc])
                ch = S.op("scalar", lambda e, hc=hc, hp=hp: e.activation(
                    out=H2H[hc], in_=H2T[hp], func=AF.Identity), inc=("s", 1))
                S.wait("vector", ch)
                S.wait("vector", h2_rd[hc])
                sb = S.op("vector", lambda e, hc=hc, hp=hp: e.tensor_tensor(
                    out=H2L[hc], in0=H2T[hp], in1=H2H[hc], op=ALU.subtract), inc=("v", 1))
                h2t_rd[hp] = sb
                h2_rd[hc] = sb

            for hc in range(NHC):
                wi = hc % 2
                S.wait("sync", w2_free[wi])
                wtk = S.op("sync", lambda e, hc=hc, wi=wi, li=li: e.dma_start(
                    out=w2ring[wi][:, :], in_=H["s_w2"][li, hc]), inc=(f"d{6 + wi}", 16))
                w2_rdy[wi] = wtk
                S.wait("tensor", wtk)
                S.wait("tensor", h2_rd[hc])
                tk = None
                for oc in range(8):
                    if hc == 0:
                        B.acquire(oc)
                    for pi_, (wof, xx) in enumerate(((0, H2H[hc]), (0, H2L[hc]), (1024, H2H[hc]))):
                        tk = S.op("tensor", lambda e, wi=wi, oc=oc, hc=hc, w=wof, x=xx,
                                  st=(hc == 0 and pi_ == 0), sp=(hc == NHC - 1 and pi_ == 2): e.matmul(
                            ps[oc][:, :], w2ring[wi][:, w + oc * 128:w + (oc + 1) * 128], x,
                            start=st, stop=sp), inc=("t", 1))
                w2_free[wi] = tk
                h2_rd[hc] = tk
            w2_done = S.tok("t")
            for oc in range(8):
                sl = ysl(oc, t0, TAU, False)
                S.wait("vector", w2_done)
                S.op("vector", lambda e, oc=oc, sl=sl: e.scalar_tensor_tensor(
                    out=sl[:, 0:511], in0=ps[oc][:, 0:511],
                    scalar=LPAR[:, 16 + oc:16 + oc + 1], in1=sl[:, 0:511],
                    op0=ALU.add, op1=ALU.add), inc=("v", 1))
                u2 = S.op("vector", lambda e, oc=oc, sl=sl, tau=tau: e.scalar_tensor_tensor(
                    out=DLF[tau % 2][oc], in0=ps[oc][:, 511:512],
                    scalar=LPAR[:, 16 + oc:16 + oc + 1], in1=sl[:, 511:512],
                    op0=ALU.add, op1=ALU.add), inc=("v", 1))
                B.release(oc, u2)

        for oc in range(8):
            S.op("vector", lambda e, oc=oc: e.tensor_copy(
                out=Y[:, oc * Z + Z - 1: oc * Z + Z], in_=DLF[(NT - 1) % 2][oc]), inc=("v", 1))
        S.barrier()

    def final():
        ar = Area()
        XT = [ar.take(2048), ar.take(2048)]
        xt_rd = [None, None]
        for d_ in range(8):
            xi = d_ % 2
            S.wait("gpsimd", xt_rd[xi])
            ld = S.op("gpsimd", lambda e, d=d_, xi=xi: e.dma_start(
                out=XT[xi], in_=H["x1scr"][d * 128:(d + 1) * 128, :]), inc=(f"gf{xi}", 16))
            S.wait("vector", ld)
            ad = S.op("vector", lambda e, d=d_, xi=xi: e.tensor_tensor(
                out=XT[xi], in0=XT[xi], in1=yt(d)[:, 1:], op=ALU.add), inc=("v", 1))
            S.wait("gpsimd", ad)
            st = S.op("gpsimd", lambda e, d=d_, xi=xi: e.dma_start(
                out=H["out"][d * 128:(d + 1) * 128, :], in_=XT[xi]), inc=(f"gf{xi}", 16))
            xt_rd[xi] = st
        f0, f1 = S.tok("gf0"), S.tok("gf1")

        def fin(e, f0=f0):
            e.wait_ge(S.sems["gf0"], f0[1])
        S.q["gpsimd"].append(fin)

        def fin2(e, f1=f1):
            e.wait_ge(S.sems["gf1"], f1[1])
        S.q["gpsimd"].append(fin2)

    phase0()
    mamba()
    for li in range(N_LAYERS):
        attn_layer(li)
        if TRUNC or TRUNC_AT is not None:
            break
        ffn_layer(li)
    if not (TRUNC or TRUNC_AT is not None):
        final()

    sem_names = ["t", "v", "s", "g", "gd", "gx", "gl", "go0", "go1", "gf0", "gf1", "dt", "pi0", "pi1", "po0", "po1", "po2", "po3", "pg"] + [f"d{i}" for i in range(10)]
    with contextlib.ExitStack() as semstack:
        sems = {n: semstack.enter_context(nc.semaphore(f"sem_{n}")) for n in sem_names}
        blk = semstack.enter_context(nc.Block())
        block = blk
        S.sems = sems

        @block.tensor
        def _(eng):
            for fn in S.q["tensor"]:
                fn(eng)

        @block.vector
        def _(eng):
            for fn in S.q["vector"]:
                fn(eng)

        @block.scalar
        def _(eng):
            for fn in S.q["scalar"]:
                fn(eng)

        @block.sync
        def _(eng):
            for fn in S.q["sync"]:
                fn(eng)

        @block.gpsimd
        def _(eng):
            for fn in S.q["gpsimd"]:
                fn(eng)

    stack.close()
    return nc


def kernel(**inputs):
    pr = _prep(inputs)
    nc = build_nc()
    shared = {k: pr[k] for k in pr if k != "x0z"}
    in_maps = []
    for c in range(NCORES):
        m = dict(shared)
        m["x0z"] = np.ascontiguousarray(pr["x0z"][c])
        in_maps.append(m)
    res = run_bass_kernel_spmd(nc, in_maps, core_ids=list(range(NCORES)))
    out = np.stack([np.ascontiguousarray(np.asarray(res.results[c]["out"]).T)
                    for c in range(NCORES)])
    return out.astype(np.float32)


# version 13 (final)
# speedup vs baseline: 1.0471x; 1.0471x over previous
import numpy as np
import concourse.bass as bass
import concourse.mybir as mybir
from concourse.bass_utils import run_bass_kernel_spmd

F32 = mybir.dt.float32
BF16 = mybir.dt.bfloat16
AF = mybir.ActivationFunctionType
ALU = mybir.AluOpType

NCORES = 8
T = 2048
TAU = 512
NT = T // TAU
WIN = 256
D = 1024
NL = 6
HID = 2816
NHC = HID // 128
Z = T + 1
SCR_COLS = 33400

ENGINES = ("tensor", "vector", "scalar", "sync", "gpsimd")


class Sched:

    def __init__(self):
        self.q = {e: [] for e in ENGINES}
        self.cnt = {}
        self.lastwait = {}
        self.sems = None

    ENG_SEM = {"vector": "v", "scalar": "s"}

    def op(self, eng, fn, inc=None):
        es = self.ENG_SEM.get(eng)
        if es is not None:
            self.wait(eng, self.tok(es))
        tok = None
        if inc is not None:
            sem, amt = inc
            self.cnt[sem] = self.cnt.get(sem, 0) + amt
            tok = (sem, self.cnt[sem])

            def wrapped(e, fn=fn, sem=sem, amt=amt):
                fn(e).then_inc(self.sems[sem], amt)
        else:
            def wrapped(e, fn=fn):
                fn(e)
        self.q[eng].append(wrapped)
        return tok

    def wait(self, eng, tok):
        if tok is None:
            return
        sem, val = tok
        if val <= 0:
            return
        key = (eng, sem)
        if self.lastwait.get(key, -1) >= val:
            return
        self.lastwait[key] = val

        def w(e, sem=sem, val=val):
            e.wait_ge(self.sems[sem], val)
        self.q[eng].append(w)

    def tok(self, sem):
        return (sem, self.cnt.get(sem, 0))

    def barrier(self):
        toks = {"tensor": self.tok("t"), "vector": self.tok("v"),
                "scalar": self.tok("s")}
        for e in ENGINES:
            for oe, tk in toks.items():
                if oe != e:
                    self.wait(e, tk)


class Banks:
    def __init__(self, S):
        self.S = S
        self.free = [None] * 8

    def acquire(self, i):
        self.S.wait("tensor", self.free[i])

    def release(self, i, tok):
        self.free[i] = tok


def _tables():
    half = np.arange(0, 64, 2, dtype=np.float64)
    inv_freq = 1.0 / 10000.0 ** (half / 64.0)
    t = np.arange(T, dtype=np.float64)
    fr = t[:, None] * inv_freq[None, :]
    base = (half + 0.4 * 64.0) / (1.4 * 64.0)
    power = (t - T // 2) / (WIN // 2)
    sc = base[None, :] ** power[:, None]
    cos, sin = np.cos(fr), np.sin(fr)

    def dup(a):
        b = np.concatenate([a.T, a.T], axis=0)
        return np.concatenate([b, b], axis=0)
    out = np.stack([dup(cos * sc), dup(sin * sc), dup(cos / sc), dup(sin / sc)])
    return np.ascontiguousarray(out.astype(np.float32))


def _consts():
    j = np.arange(128)[:, None]
    i = np.arange(128)[None, :]
    tge = (j >= i).astype(np.float32)
    tle = (j <= i).astype(np.float32)
    R = np.zeros((128, 128), np.float32)
    for blk in range(2):
        for m in range(64):
            pi = m + 32 if m < 32 else m - 32
            R[blk * 64 + pi, blk * 64 + m] = -1.0 if m < 32 else 1.0
    bd = np.zeros((128, 128), np.float32)
    bd[:64, :64] = 1.0
    bd[64:, 64:] = 1.0
    mean = np.full((128, 128), 1.0 / 1024.0, np.float32)
    ones = np.ones((128, 128), np.float32)
    return np.ascontiguousarray(np.stack([tge, tle, R, bd, mean, ones]))


def _prep(inp):
    f = np.float32
    pr = {}
    x = np.asarray(inp["L0_attn_x"], f)
    xz = np.zeros((NCORES, D, Z), f)
    for c in range(NCORES):
        xz[c, :, 1:] = x[c].T
    pr["x0z"] = xz

    W_in = np.asarray(inp["L0_attn_W_in"], f)
    pr["w_int"] = np.ascontiguousarray(
        W_in.T.reshape(8, 128, 2, 1024).transpose(2, 0, 1, 3))
    pr["w_st"] = np.ascontiguousarray(np.asarray(inp["L0_attn_W_s"], f).T.reshape(8, 128, 1024))
    pr["w_ot"] = np.ascontiguousarray(np.asarray(inp["L0_attn_W_o"], f).T.reshape(8, 128, 1024))
    mpar = np.zeros((128, 32), f)
    mpar[:, 0:16] = np.asarray(inp["L0_attn_b_in"], f).reshape(16, 128).T
    mpar[:, 16:24] = np.asarray(inp["L0_attn_b_s"], f).reshape(8, 128).T
    mpar[:, 24:32] = np.asarray(inp["L0_attn_b_o"], f).reshape(8, 128).T
    pr["mpar"] = mpar

    Wqkv = np.asarray(inp["L0_ffn_Wqkv"], f)
    qk = np.empty((NL, 8, 8, 128, 256), f)
    for p in range(8):
        qk[:, p, :, :, 0:128] = Wqkv[:, :, p * 128:(p + 1) * 128].reshape(NL, 8, 128, 128)
        qk[:, p, :, :, 128:256] = Wqkv[:, :, 1024 + p * 128:1024 + (p + 1) * 128].reshape(NL, 8, 128, 128)
    pr["wqkvqk"] = np.ascontiguousarray(qk)
    pr["wqkvv"] = np.ascontiguousarray(Wqkv[:, :, 2048:].reshape(NL, 8, 128, 1024))

    Wo = np.asarray(inp["L0_ffn_Wo"], f)
    pr["wo"] = np.ascontiguousarray(
        Wo.reshape(NL, 8, 128, 8, 128).transpose(0, 3, 1, 2, 4))

    W1 = np.asarray(inp["L0_ffn_W1"], f)
    a = np.zeros((NL, D, HID), f)
    g = np.zeros((NL, D, HID), f)
    a[:, :, :2730] = W1[:, :, :2730]
    g[:, :, :2730] = W1[:, :, 2730:]
    w1r = np.empty((NL, NHC, 8, 128, 256), f)
    w1r[..., 0:128] = a.reshape(NL, 8, 128, NHC, 128).transpose(0, 3, 1, 2, 4)
    w1r[..., 128:256] = g.reshape(NL, 8, 128, NHC, 128).transpose(0, 3, 1, 2, 4)
    pr["w1r"] = np.ascontiguousarray(w1r)

    W2 = np.asarray(inp["L0_ffn_W2"], f)
    w2p = np.zeros((NL, HID, D), f)
    w2p[:, :2730, :] = W2
    pr["w2p"] = np.ascontiguousarray(w2p.reshape(NL, NHC, 128, D))

    lpar = np.zeros((NL, 128, 70), f)
    lpar[:, :, 0:8] = np.asarray(inp["L0_ffn_ln_g"], f).reshape(NL, 8, 128).transpose(0, 2, 1)
    lpar[:, :, 8:16] = np.asarray(inp["L0_ffn_ln_b"], f).reshape(NL, 8, 128).transpose(0, 2, 1)
    lpar[:, :, 16:24] = np.asarray(inp["L0_ffn_b2"], f).reshape(NL, 8, 128).transpose(0, 2, 1)
    qs = np.asarray(inp["L0_ffn_q_scale"], f)
    ks = np.asarray(inp["L0_ffn_k_scale"], f)
    lpar[:, :, 24] = np.concatenate([qs, qs], axis=1)
    lpar[:, :, 25] = np.concatenate([ks, ks], axis=1)
    b1 = np.asarray(inp["L0_ffn_b1"], f)
    b1a = np.zeros((NL, HID), f)
    b1g = np.zeros((NL, HID), f)
    b1a[:, :2730] = b1[:, :2730]
    b1g[:, :2730] = b1[:, 2730:]
    lpar[:, :, 26:48] = b1a.reshape(NL, NHC, 128).transpose(0, 2, 1)
    lpar[:, :, 48:70] = b1g.reshape(NL, NHC, 128).transpose(0, 2, 1)
    pr["lpar"] = lpar

    pr["tab"] = _tables()
    pr["con"] = _consts()
    return pr


N_LAYERS = NL
DBG = False
TRUNC = False
TRUNC_AT = None


def build_nc():
    nc = bass.Bass("TRN2", target_bir_lowering=False, detect_race_conditions=not DBG)
    H = {}
    H["x0z"] = nc.dram_tensor("x0z", [D, Z], F32, kind="ExternalInput").ap()
    H["w_int"] = nc.dram_tensor("w_int", [2, 8, 128, 1024], F32, kind="ExternalInput").ap()
    H["w_st"] = nc.dram_tensor("w_st", [8, 128, 1024], F32, kind="ExternalInput").ap()
    H["w_ot"] = nc.dram_tensor("w_ot", [8, 128, 1024], F32, kind="ExternalInput").ap()
    H["mpar"] = nc.dram_tensor("mpar", [128, 32], F32, kind="ExternalInput").ap()
    H["wqkvqk"] = nc.dram_tensor("wqkvqk", [NL, 8, 8, 128, 256], F32, kind="ExternalInput").ap()
    H["wqkvv"] = nc.dram_tensor("wqkvv", [NL, 8, 128, 1024], F32, kind="ExternalInput").ap()
    H["wo"] = nc.dram_tensor("wo", [NL, 8, 8, 128, 128], F32, kind="ExternalInput").ap()
    H["w1r"] = nc.dram_tensor("w1r", [NL, NHC, 8, 128, 256], F32, kind="ExternalInput").ap()
    H["w2p"] = nc.dram_tensor("w2p", [NL, NHC, 128, D], F32, kind="ExternalInput").ap()
    H["lpar"] = nc.dram_tensor("lpar", [NL, 128, 70], F32, kind="ExternalInput").ap()
    H["tab"] = nc.dram_tensor("tab", [4, 128, T], F32, kind="ExternalInput").ap()
    H["con"] = nc.dram_tensor("con", [6, 128, 128], F32, kind="ExternalInput").ap()
    H["x1scr"] = nc.dram_tensor("x1scr", [D, T], F32, kind="Internal").ap()
    H["s_w1"] = nc.dram_tensor("s_w1", [NL, NHC, 8, 128, 512], BF16, kind="Internal").ap()
    H["s_w2"] = nc.dram_tensor("s_w2", [NL, NHC, 128, 2048], BF16, kind="Internal").ap()
    if DBG:
        H["dbg"] = nc.dram_tensor("dbg", [128, 16384], F32, kind="ExternalOutput").ap()
    H["out"] = nc.dram_tensor("out", [D, T], F32, kind="ExternalOutput").ap()

    import contextlib
    stack = contextlib.ExitStack()

    def alloc(name, cols, dt=F32):
        return stack.enter_context(nc.sbuf_tensor(name, [128, cols], dt))

    Y = alloc("Y", 8 * Z)
    CON = alloc("CON", 6 * 128)
    TAB = alloc("TAB", 4 * TAU)
    LPAR = alloc("LPAR", 70)
    MPAR = alloc("MPAR", 32)
    EPSA = alloc("EPSA", 1)
    SCR = alloc("SCR", SCR_COLS)
    ps = [stack.enter_context(nc.psum_tensor(f"ps{i}", [128, 512], F32)) for i in range(8)]

    S = Sched()
    B = Banks(S)

    def yt(d):
        return Y[:, d * Z:(d + 1) * Z]

    def ysl(d, t0, n, shifted):
        off = t0 if (shifted and d >= 4) else t0 + 1
        return Y[:, d * Z + off: d * Z + off + n]

    def con(i):
        return CON[:, i * 128:(i + 1) * 128]
    TGE, TLE, RPERM, BD64, MEAN, ONES = (con(i) for i in range(6))

    class Area:
        def __init__(self, off=0):
            self.off = off

        def take(self, cols):
            a = SCR[:, self.off:self.off + cols]
            self.off += cols
            assert self.off <= SCR_COLS, f"SCR overflow {self.off}"
            return a

        def takeb(self, bcols):
            cols = (bcols + 1) // 2
            a = SCR[:, self.off:self.off + cols].bitcast(BF16)
            self.off += cols
            assert self.off <= SCR_COLS, f"SCR overflow {self.off}"
            return a

    def phase0():
        for d_ in range(8):
            S.op("gpsimd", lambda e, d=d_: e.dma_start(
                out=yt(d), in_=H["x0z"][d * 128:(d + 1) * 128, :]), inc=("g", 16))
        S.op("gpsimd", lambda e: e.dma_start(
            out=CON[:, :].rearrange("p (k n) -> p k n", k=6),
            in_=H["con"].rearrange("k p n -> p k n")), inc=("g", 16))
        S.op("gpsimd", lambda e: e.dma_start(out=MPAR[:, :], in_=H["mpar"]), inc=("g", 16))
        ep = S.op("vector", lambda e: e.memset(EPSA[:, 0:1], 1e-5), inc=("v", 1))
        S.wait("scalar", ep)
        g0 = S.tok("g")
        for e in ("tensor", "vector", "scalar", "sync"):
            S.wait(e, g0)

    class LNCtx:

        def __init__(self, ar):
            self.sqs = [ar.take(512), ar.take(512)]
            self.MS = ar.take(512)
            self.VAR = ar.take(512)
            self.RS = ar.take(512)
            self.MU = ar.take(512)
            self.sq_rd = [None, None]
            self.ms_rd = None
            self.mu_rd = None
            self.rs_rd = None

    def ln_compute(ctx, t0, shifted, mu_bank, sq_bank):
        B.acquire(mu_bank)
        B.acquire(sq_bank)
        mtok = stok = None
        sqtoks = []
        for d_ in range(8):
            sl = ysl(d_, t0, TAU, shifted)
            buf = ctx.sqs[d_ % 2]
            if d_ < 2:
                S.wait("scalar", ctx.sq_rd[d_ % 2])
                sqtoks.append(S.op("scalar", lambda e, sl=sl, b=buf: e.activation(
                    out=b, in_=sl, func=AF.Square), inc=("s", 1)))
            mtok = S.op("tensor", lambda e, sl=sl, st=(d_ == 0), sp=(d_ == 7): e.matmul(
                ps[mu_bank][:, :], MEAN, sl, start=st, stop=sp), inc=("t", 1))
        for d_ in range(8):
            sl = ysl(d_, t0, TAU, shifted)
            buf = ctx.sqs[d_ % 2]
            if d_ >= 2:
                S.wait("scalar", ctx.sq_rd[d_ % 2])
                sqtoks.append(S.op("scalar", lambda e, sl=sl, b=buf: e.activation(
                    out=b, in_=sl, func=AF.Square), inc=("s", 1)))
            S.wait("tensor", sqtoks[d_])
            stok = S.op("tensor", lambda e, b=buf, st=(d_ == 0), sp=(d_ == 7): e.matmul(
                ps[sq_bank][:, :], MEAN, b, start=st, stop=sp), inc=("t", 1))
            ctx.sq_rd[d_ % 2] = stok
        S.wait("scalar", mtok)
        S.wait("scalar", ctx.ms_rd)
        t1 = S.op("scalar", lambda e: e.activation(
            out=ctx.MS, in_=ps[mu_bank][:, :], func=AF.Square), inc=("s", 1))
        S.wait("scalar", ctx.mu_rd)
        tmu = S.op("scalar", lambda e: e.activation(
            out=ctx.MU, in_=ps[mu_bank][:, :], func=AF.Identity), inc=("s", 1))
        S.wait("vector", stok)
        S.wait("vector", t1)
        S.wait("vector", ctx.rs_rd)
        t2 = S.op("vector", lambda e: e.scalar_tensor_tensor(
            out=ctx.VAR, in0=ps[sq_bank][:, :], scalar=1.0, in1=ctx.MS,
            op0=ALU.mult, op1=ALU.subtract), inc=("v", 1))
        ctx.ms_rd = t2
        B.release(mu_bank, tmu)
        B.release(sq_bank, t2)
        S.wait("scalar", t2)
        tln = S.op("scalar", lambda e: e.activation(
            out=ctx.VAR, in_=ctx.VAR, func=AF.Ln, bias=EPSA[:, 0:1]), inc=("s", 1))
        S.wait("scalar", tln)
        trs = S.op("scalar", lambda e: e.activation(
            out=ctx.RS, in_=ctx.VAR, func=AF.Exp, scale=-0.5), inc=("s", 1))
        return tmu, trs

    PROL_BASE = 31352
    prol_done = [None] * NL
    prol_state = {"tin_rd": [None, None], "thi_rd": [None, None], "tlo_rd": [None, None],
                  "cast": [None, None], "sub": [None, None]}

    def emit_split_unit(ar_off, ui, src_ap, hi_out, lo_out, ncols):
        st = prol_state
        slot = ui % 2
        TIN = SCR[:, ar_off + slot * 512: ar_off + slot * 512 + ncols]
        THI = SCR[:, ar_off + 1024 + slot * 256: ar_off + 1024 + slot * 256 + (ncols + 1) // 2].bitcast(BF16)
        TLO = SCR[:, ar_off + 1536 + slot * 256: ar_off + 1536 + slot * 256 + (ncols + 1) // 2].bitcast(BF16)
        itok = S.op("gpsimd", lambda e, a=src_ap, t=TIN: e.dma_start(out=t, in_=a),
                    inc=(f"pi{slot}", 16))
        st["tin_rd"][slot] = itok
        S.wait("gpsimd", itok)
        S.wait("gpsimd", st["thi_rd"][slot])
        ct = S.op("gpsimd", lambda e, t=TIN, h=THI: e.tensor_copy(
            out=h, in_=t), inc=("pg", 1))
        st["cast"][slot] = ct
        S.wait("gpsimd", ct)
        S.wait("gpsimd", st["tlo_rd"][slot])
        sb = S.op("gpsimd", lambda e, t=TIN, h=THI, l=TLO: e.tensor_tensor(
            out=l, in0=t, in1=h, op=ALU.subtract), inc=("pg", 1))
        st["sub"][slot] = sb
        S.wait("gpsimd", sb)
        ht = S.op("gpsimd", lambda e, h=THI, o=hi_out: e.dma_start(out=o, in_=h),
                  inc=(f"po{slot}", 16))
        st["thi_rd"][slot] = ht
        lt = S.op("gpsimd", lambda e, l=TLO, o=lo_out: e.dma_start(out=o, in_=l),
                  inc=(f"po{2 + slot}", 16))
        st["tlo_rd"][slot] = lt
        return lt

    def emit_split_ffn(li, ar_off, quarter=None):
        ui = [0]
        hcs = range(NHC) if quarter is None else range(
            quarter * 6, min(NHC, (quarter + 1) * 6))
        for hc in hcs:
            w1v = H["w1r"][li, hc].rearrange("d p n -> p d n")
            s1v = H["s_w1"][li, hc].rearrange("d p n -> p d n")
            for dh in range(2):
                dsl = slice(dh * 4, dh * 4 + 4)
                emit_split_unit(ar_off, ui[0], w1v[:, dsl, 0:128],
                                s1v[:, dsl, 0:128], s1v[:, dsl, 256:384], 512)
                ui[0] += 1
                emit_split_unit(ar_off, ui[0], w1v[:, dsl, 128:256],
                                s1v[:, dsl, 128:256], s1v[:, dsl, 384:512], 512)
                ui[0] += 1
            for wh in range(2):
                emit_split_unit(ar_off, ui[0], H["w2p"][li, hc][:, wh * 512:(wh + 1) * 512],
                                H["s_w2"][li, hc][:, wh * 512:wh * 512 + 512],
                                H["s_w2"][li, hc][:, 1024 + wh * 512:1024 + wh * 512 + 512], 512)
                ui[0] += 1
        if quarter is None or quarter == 3:
            prol_done[li] = [S.tok(f"po{i}") for i in range(4)] + [S.tok("pg")]

    def mamba():
        ar = Area()
        LNB = [ar.take(512) for _ in range(8)]
        GT = [ar.take(512) for _ in range(8)]
        VL = [ar.take(512) for _ in range(8)]
        AB = [ar.take(513) for _ in range(8)]
        BB = [ar.take(513) for _ in range(8)]
        TMP = [ar.take(512), ar.take(512)]
        ring = [ar.take(1024) for _ in range(4)]
        lctx = LNCtx(ar)
        ring_free = [None] * 4
        ring_rdy = [None] * 4
        rslot = [0]

        for e8 in range(8):
            S.op("vector", lambda e, a=AB[e8]: e.memset(a[:, 0:1], 0.0), inc=("v", 1))

        def load_slab(hbm_ap):
            i = rslot[0] % 4
            rslot[0] += 1
            S.wait("sync", ring_free[i])
            tok = S.op("sync", lambda e, h=hbm_ap, sl=ring[i]: e.dma_start(
                out=sl[:, :], in_=h), inc=(f"d{i}", 16))
            ring_rdy[i] = tok
            return i

        lnb_w = [None] * 8
        lnb_rd = [None] * 8
        gt_w = [None] * 8
        gt_rd = [None] * 8
        vl_w = [None] * 8
        vl_rd = [None] * 8
        bb_rd = [None] * 8
        tmp_rd = [None, None]
        scan_toks = [None] * 8

        for tau in range(NT):
            t0 = tau * TAU
            if N_LAYERS > 0:
                emit_split_ffn(0, PROL_BASE, quarter=tau)
            tmu, trs = ln_compute(lctx, t0, False, 0, 1)
            mu_cons = rs_cons = None
            for d_ in range(8):
                sl = ysl(d_, t0, TAU, False)
                S.wait("vector", lnb_rd[d_])
                S.wait("vector", tmu)
                S.op("vector", lambda e, sl=sl, d=d_: e.scalar_tensor_tensor(
                    out=LNB[d], in0=sl, scalar=1.0, in1=lctx.MU,
                    op0=ALU.mult, op1=ALU.subtract), inc=("v", 1))
                S.wait("vector", trs)
                lt = S.op("vector", lambda e, d=d_: e.tensor_tensor(
                    out=LNB[d], in0=LNB[d], in1=lctx.RS, op=ALU.mult), inc=("v", 1))
                lnb_w[d_] = lt
                mu_cons = rs_cons = lt
            lctx.mu_rd = mu_cons
            lctx.rs_rd = rs_cons

            for eg in range(2):
                banktok = [None] * 8
                for d_ in range(8):
                    si = load_slab(H["w_int"][eg, d_])
                    S.wait("tensor", ring_rdy[si])
                    S.wait("tensor", lnb_w[d_])
                    tk = None
                    for e8 in range(8):
                        if d_ == 0:
                            B.acquire(e8)
                        tk = S.op("tensor", lambda e, si=si, e8=e8, d=d_: e.matmul(
                            ps[e8][:, :], ring[si][:, e8 * 128:(e8 + 1) * 128], LNB[d],
                            start=(d == 0), stop=(d == 7)), inc=("t", 1))
                        banktok[e8] = tk
                    ring_free[si] = tk
                    lnb_rd[d_] = tk
                for e8 in range(8):
                    S.wait("scalar", banktok[e8])
                    if eg == 0:
                        S.wait("scalar", gt_rd[e8])
                        ev = S.op("scalar", lambda e, e8=e8: e.activation(
                            out=GT[e8], in_=ps[e8][:, :], func=AF.Sigmoid,
                            bias=MPAR[:, e8:e8 + 1]), inc=("s", 1))
                        gt_w[e8] = ev
                    else:
                        S.wait("scalar", vl_rd[e8])
                        ev = S.op("scalar", lambda e, e8=e8: e.activation(
                            out=VL[e8], in_=ps[e8][:, :], func=AF.Identity,
                            bias=MPAR[:, 8 + e8:8 + e8 + 1]), inc=("s", 1))
                        vl_w[e8] = ev
                    B.release(e8, ev)

            banktok = [None] * 8
            for d_ in range(8):
                si = load_slab(H["w_st"][d_])
                S.wait("tensor", ring_rdy[si])
                S.wait("tensor", vl_w[d_])
                tk = None
                for e8 in range(8):
                    if d_ == 0:
                        B.acquire(e8)
                    tk = S.op("tensor", lambda e, si=si, e8=e8, d=d_: e.matmul(
                        ps[e8][:, :], ring[si][:, e8 * 128:(e8 + 1) * 128], VL[d],
                        start=(d == 0), stop=(d == 7)), inc=("t", 1))
                    banktok[e8] = tk
                ring_free[si] = tk
                vl_rd[d_] = tk
            for e8 in range(8):
                tm = TMP[e8 % 2]
                S.wait("scalar", banktok[e8])
                S.wait("scalar", tmp_rd[e8 % 2])
                ev = S.op("scalar", lambda e, e8=e8, tm=tm: e.activation(
                    out=tm, in_=ps[e8][:, :], func=AF.Tanh,
                    bias=MPAR[:, 16 + e8:16 + e8 + 1]), inc=("s", 1))
                B.release(e8, ev)
                S.wait("vector", bb_rd[e8])
                if tau > 0:
                    S.op("vector", lambda e, e8=e8: e.tensor_copy(
                        out=BB[e8][:, 0:1], in_=BB[e8][:, 512:513]), inc=("v", 1))
                else:
                    S.op("vector", lambda e, e8=e8: e.memset(BB[e8][:, 0:1], 0.0), inc=("v", 1))
                S.wait("vector", ev)
                S.wait("vector", gt_w[e8])
                btk = S.op("vector", lambda e, e8=e8, tm=tm: e.tensor_tensor(
                    out=BB[e8][:, 1:513], in0=tm, in1=GT[e8], op=ALU.mult), inc=("v", 1))
                tmp_rd[e8 % 2] = btk
                at = S.op("vector", lambda e, e8=e8: e.tensor_scalar(
                    out=AB[e8][:, 1:513], in0=GT[e8], scalar1=-1.0, scalar2=1.0,
                    op0=ALU.mult, op1=ALU.add), inc=("v", 1))
                gt_rd[e8] = at
                sc = S.op("vector", lambda e, e8=e8: e.tensor_tensor_scan(
                    out=BB[e8][:, :], data0=AB[e8][:, :], data1=BB[e8][:, :],
                    initial=0.0, op0=ALU.mult, op1=ALU.add), inc=("v", 1))
                scan_toks[e8] = sc

            for d_ in range(8):
                si = load_slab(H["w_ot"][d_])
                S.wait("tensor", ring_rdy[si])
                S.wait("tensor", scan_toks[d_])
                tk = None
                for oc in range(8):
                    if d_ == 0:
                        B.acquire(oc)
                    tk = S.op("tensor", lambda e, si=si, oc=oc, d=d_: e.matmul(
                        ps[oc][:, :], ring[si][:, oc * 128:(oc + 1) * 128], BB[d][:, 1:513],
                        start=(d == 0), stop=(d == 7)), inc=("t", 1))
                ring_free[si] = tk
                bb_rd[d_] = tk
            wo_done = S.tok("t")
            for oc in range(8):
                sl = ysl(oc, t0, TAU, False)
                S.wait("vector", wo_done)
                u1 = S.op("vector", lambda e, oc=oc, sl=sl: e.scalar_tensor_tensor(
                    out=sl, in0=ps[oc][:, :], scalar=MPAR[:, 24 + oc:24 + oc + 1],
                    in1=sl, op0=ALU.add, op1=ALU.add), inc=("v", 1))
                B.release(oc, u1)
                u2 = S.op("vector", lambda e, oc=oc, sl=sl: e.tensor_tensor(
                    out=sl, in0=sl, in1=LNB[oc], op=ALU.add), inc=("v", 1))
                lnb_rd[oc] = u2

        S.barrier()
        for d_ in range(8):
            S.op("gpsimd", lambda e, d=d_: e.dma_start(
                out=H["x1scr"][d * 128:(d + 1) * 128, :], in_=yt(d)[:, 1:]), inc=("gx", 16))
        gsave = S.tok("gx")
        for tau in range(NT):
            t0 = tau * TAU
            tmu, trs = ln_compute(lctx, t0, False, 0, 1)
            S.wait("vector", gsave)
            last = None
            for d_ in range(8):
                sl = ysl(d_, t0, TAU, False)
                S.wait("vector", tmu)
                S.op("vector", lambda e, sl=sl: e.scalar_tensor_tensor(
                    out=sl, in0=sl, scalar=1.0, in1=lctx.MU,
                    op0=ALU.mult, op1=ALU.subtract), inc=("v", 1))
                S.wait("vector", trs)
                last = S.op("vector", lambda e, sl=sl: e.tensor_tensor(
                    out=sl, in0=sl, in1=lctx.RS, op=ALU.mult), inc=("v", 1))
            lctx.mu_rd = last
            lctx.rs_rd = last
        S.barrier()

    def attn_layer(li):
        ar = Area()
        QB = ar.take(4096)
        KR = ar.take(8 * 3 * 256)
        VT = [ar.take(1040) for _ in range(6)]
        OSB = ar.take(4096)
        PT = [ar.take(256) for _ in range(8)]
        SU = ar.take(512)
        RB = ar.take(512)
        OTT = [ar.take(256), ar.take(256)]
        SQ2 = ar.take(512)
        SS = ar.take(512)
        Q0 = ar.take(512)
        T1 = ar.take(512)
        DL = [[ar.take(1) for _ in range(8)] for _ in range(2)]
        wring = [ar.take(2048), ar.take(2048)]
        vring = [ar.take(1024)]
        ar.take(PROL_BASE - ar.off)
        oring = [ar.take(1024), ar.take(1024)]
        lctx_unused = None
        wring_free = [None, None]
        wring_rdy = [None, None]
        vring_free = [None, None]
        vring_rdy = [None, None]
        oring_free = [None, None]
        oring_rdy = [None, None]

        ltok = S.op("gpsimd", lambda e, li=li: e.dma_start(out=LPAR[:, :], in_=H["lpar"][li]), inc=("gl", 16))
        S.wait("vector", ltok)
        S.wait("scalar", ltok)
        QS = LPAR[:, 24:25]
        KS = LPAR[:, 25:26]

        kr_wr = [None] * 3
        kr_rd = [None] * 3
        vt_wr = [None] * 6
        vt_rd = [None] * 6
        pt_wr = [None] * 8
        pt_rd = [None] * 8
        qp = {"sq2_rd": None, "q0_rd": None, "ss_srd": None}
        su_rd = None
        rb_rd = None
        ott_rd = [None, None]
        qb_rd = None
        osb_rd = None
        tab_rd = None

        for idx, half in ((0, 1), (3, 0), (4, 1), (7, 0)):
            tk = S.op("vector", lambda e, i=idx, h=half: e.memset(
                PT[i][:, h * 128:(h + 1) * 128], 0.0), inc=("v", 1))
            pt_wr[idx] = tk

        def q_proc(bankQ, mmtok, is_q, dests):
            nonlocal qp
            sc_tab = QS if is_q else KS
            A_sl = TAB[:, 0:TAU] if is_q else TAB[:, 2 * TAU:3 * TAU]
            S_sl = TAB[:, TAU:2 * TAU] if is_q else TAB[:, 3 * TAU:4 * TAU]
            S.wait("scalar", mmtok)
            S.wait("scalar", qp["sq2_rd"])
            t1 = S.op("scalar", lambda e, b=bankQ: e.activation(
                out=SQ2, in_=ps[b][:, :], func=AF.Square), inc=("s", 1))
            S.wait("tensor", t1)
            B.acquire(4)
            t2 = S.op("tensor", lambda e: e.matmul(
                ps[4][:, :], BD64, SQ2, start=True, stop=True), inc=("t", 1))
            qp["sq2_rd"] = t2
            S.wait("vector", t2)
            S.wait("vector", qp["ss_srd"])
            t3 = S.op("vector", lambda e: e.tensor_scalar(
                out=SS, in0=ps[4][:, :], scalar1=1e-18, scalar2=None, op0=ALU.max), inc=("v", 1))
            B.release(4, t3)
            S.wait("scalar", t3)
            tl = S.op("scalar", lambda e: e.activation(out=SS, in_=SS, func=AF.Ln), inc=("s", 1))
            S.wait("scalar", tl)
            t5 = S.op("scalar", lambda e: e.activation(
                out=SS, in_=SS, func=AF.Exp, scale=-0.5), inc=("s", 1))
            qp["ss_srd"] = t5
            S.wait("vector", t5)
            S.wait("vector", qp["q0_rd"])
            t6 = S.op("vector", lambda e, b=bankQ: e.tensor_tensor(
                out=Q0, in0=ps[b][:, :], in1=SS, op=ALU.mult), inc=("v", 1))
            B.release(bankQ, t6)
            S.op("vector", lambda e, sc=sc_tab: e.tensor_scalar(
                out=Q0, in0=Q0, scalar1=sc, scalar2=None, op0=ALU.mult), inc=("v", 1))
            t7 = S.tok("v")
            S.wait("tensor", t7)
            B.acquire(5)
            t8 = S.op("tensor", lambda e: e.matmul(
                ps[5][:, :], RPERM, Q0, start=True, stop=True), inc=("t", 1))
            qp["q0_rd"] = t8
            S.op("vector", lambda e, A=A_sl: e.tensor_tensor(
                out=T1, in0=Q0, in1=A, op=ALU.mult), inc=("v", 1))
            S.wait("vector", t8)
            t10 = S.op("vector", lambda e, Ss=S_sl: e.tensor_tensor(
                out=Q0, in0=ps[5][:, :], in1=Ss, op=ALU.mult), inc=("v", 1))
            B.release(5, t10)
            toks = []
            off = 0
            for (ap_, n_) in dests:
                tk = S.op("vector", lambda e, a=ap_, o=off, n=n_: e.tensor_tensor(
                    out=a, in0=T1[:, o:o + n], in1=Q0[:, o:o + n], op=ALU.add), inc=("v", 1))
                toks.append(tk)
                off += n_
            return toks

        for tau in range(NT):
            t0 = tau * TAU
            S.wait("sync", tab_rd)
            ttok = None
            for k in range(4):
                ttok = S.op("sync", lambda e, k=k, t0=t0: e.dma_start(
                    out=TAB[:, k * TAU:(k + 1) * TAU], in_=H["tab"][k, :, t0:t0 + TAU]), inc=("dt", 16))
            S.wait("vector", ttok)

            S.wait("vector", qb_rd)
            w0, w1 = 2 * tau, 2 * tau + 1
            ktoks = [None, None]
            for p in range(8):
                wi = p % 2
                S.wait("sync", wring_free[wi])
                wtk = S.op("sync", lambda e, p=p, wi=wi, li=li: e.dma_start(
                    out=wring[wi][:, :].rearrange("p (d n) -> p d n", d=8),
                    in_=H["wqkvqk"][li, p].rearrange("d p n -> p d n")), inc=(f"d{4 + wi}", 16))
                wring_rdy[wi] = wtk
                S.wait("tensor", wtk)
                bq, bkk = (0, 1) if p % 2 == 0 else (2, 3)
                B.acquire(bq)
                B.acquire(bkk)
                tk = None
                qtk = None
                for d_ in range(8):
                    sl = ysl(d_, t0, TAU, True)
                    qtk = S.op("tensor", lambda e, wi=wi, d=d_, sl=sl, bq=bq: e.matmul(
                        ps[bq][:, :], wring[wi][:, d * 256:d * 256 + 128], sl,
                        start=(d == 0), stop=(d == 7)), inc=("t", 1))
                for d_ in range(8):
                    sl = ysl(d_, t0, TAU, True)
                    tk = S.op("tensor", lambda e, wi=wi, d=d_, sl=sl, bkk=bkk: e.matmul(
                        ps[bkk][:, :], wring[wi][:, d * 256 + 128:(d + 1) * 256], sl,
                        start=(d == 0), stop=(d == 7)), inc=("t", 1))
                wring_free[wi] = tk
                if DBG and li == 0 and tau == 0 and p == 0:
                    S.wait("scalar", qtk)
                    S.wait("scalar", tk)
                    dq = S.op("scalar", lambda e: e.activation(out=T1[:, :], in_=ps[0][:, :], func=AF.Identity), inc=("s", 1))
                    S.wait("gpsimd", dq)
                    S.wait("gpsimd", S.tok("gd"))
                    dqd = S.op("gpsimd", lambda e: e.dma_start(out=H["dbg"][:, 0:512], in_=T1[:, :]), inc=("gd", 16))
                    S.wait("scalar", dqd)
                    dk = S.op("scalar", lambda e: e.activation(out=T1[:, :], in_=ps[1][:, :], func=AF.Identity), inc=("s", 1))
                    S.wait("gpsimd", dk)
                    S.wait("gpsimd", S.tok("gd"))
                    dkd = S.op("gpsimd", lambda e: e.dma_start(out=H["dbg"][:, 512:1024], in_=T1[:, :]), inc=("gd", 16))
                    S.wait("gpsimd", S.tok("gd"))
                    S.op("gpsimd", lambda e: e.dma_start(out=H["dbg"][:, 1024:1536], in_=Y[:, 0:512]), inc=("gd", 16))
                    S.wait("gpsimd", S.tok("gd"))
                    S.op("gpsimd", lambda e: e.dma_start(out=H["dbg"][:, 1536:2048], in_=Y[:, 4 * Z:4 * Z + 512]), inc=("gd", 16))
                    S.wait("gpsimd", S.tok("gd"))
                    S.op("gpsimd", lambda e: e.dma_start(out=H["dbg"][:, 2048:4096], in_=wring[wi][:, :]), inc=("gd", 16))
                    S.wait("vector", dkd)
                q_proc(bq, qtk, True, [(QB[:, p * 512:(p + 1) * 512], 512)])
                s0 = KR[:, p * 768 + (w0 % 3) * 256: p * 768 + (w0 % 3) * 256 + 256]
                s1 = KR[:, p * 768 + (w1 % 3) * 256: p * 768 + (w1 % 3) * 256 + 256]
                S.wait("vector", kr_rd[w0 % 3])
                S.wait("vector", kr_rd[w1 % 3])
                kt = q_proc(bkk, tk, False, [(s0, 256), (s1, 256)])
                ktoks = kt
                if TRUNC:
                    return

            kr_wr[w0 % 3] = ktoks[0]
            kr_wr[w1 % 3] = ktoks[1]
            tab_rd = S.tok("v")
            if TRUNC_AT == (li, tau, "a2"):
                return
            if DBG and li == 0 and tau == 0:
                S.wait("gpsimd", S.tok("v"))
                S.wait("gpsimd", S.tok("gd"))
                S.op("gpsimd", lambda e: e.dma_start(
                    out=H["dbg"][:, 5632:6144], in_=KR[:, 0:512]), inc=("gd", 16))
                S.wait("gpsimd", S.tok("gd"))
                S.op("gpsimd", lambda e: e.dma_start(
                    out=H["dbg"][:, 6144:6656], in_=QB[:, 0:512]), inc=("gd", 16))

            for sci in range(4):
                slot = (tau * 4 + sci) % 6
                S.wait("vector", vt_rd[slot])
                S.op("vector", lambda e, s=slot: e.memset(
                    VT[s][:, :].rearrange("p (h n) -> p h n", n=65)[:, :, 64:65], 1.0), inc=("v", 1))
            vmem = S.tok("v")
            banktok = [None] * 8
            for d_ in range(8):
                for hh in range(2):
                    S.wait("sync", vring_free[hh])
                    vtk = S.op("sync", lambda e, d=d_, hh=hh, li=li: e.dma_start(
                        out=vring[0][:, hh * 512:(hh + 1) * 512],
                        in_=H["wqkvv"][li, d][:, hh * 512:(hh + 1) * 512]),
                        inc=(f"d{6 if hh == 0 else 9}", 16))
                    vring_rdy[hh] = vtk
                    S.wait("tensor", vtk)
                    tk = None
                    for sci in range(4):
                        tsub0 = t0 + sci * 128
                        sl = ysl(d_, tsub0, 128, True)
                        bk = sci * 2 + hh
                        if d_ == 0:
                            B.acquire(bk)
                        tk = S.op("tensor", lambda e, sl=sl, hh=hh, bk=bk, d=d_: e.matmul(
                            ps[bk][:, :], sl, vring[0][:, hh * 512:(hh + 1) * 512],
                            start=(d == 0), stop=(d == 7)), inc=("t", 1))
                        banktok[bk] = tk
                    vring_free[hh] = tk
            for sci in range(4):
                slot = (tau * 4 + sci) % 6
                ev = None
                for hh in range(2):
                    bk = sci * 2 + hh
                    S.wait("scalar", banktok[bk])
                    S.wait("scalar", vmem)
                    S.wait("scalar", vt_rd[slot])
                    ev = S.op("scalar", lambda e, s=slot, hh=hh, bk=bk: e.activation(
                        out=VT[s][:, :].rearrange("p (h n) -> p h n", n=65)[:, hh * 8:(hh + 1) * 8, 0:64],
                        in_=ps[bk][:, :].rearrange("p (h n) -> p h n", n=64),
                        func=AF.Identity), inc=("s", 1))
                    B.release(bk, ev)
                vt_wr[slot] = ev
            if TRUNC_AT == (li, tau, "a3"):
                return
            qkv_done = S.tok("t")

            if tau > 0:
                for oc in range(8):
                    S.wait("vector", qkv_done)
                    S.op("vector", lambda e, oc=oc, tp=t0 - 1, tau=tau: e.tensor_copy(
                        out=Y[:, oc * Z + 1 + tp: oc * Z + 2 + tp],
                        in_=DL[(tau - 1) % 2][oc]), inc=("v", 1))

            a4v = S.tok("v")
            S.wait("tensor", a4v)
            pending = None
            osbA_tok = [[None] * 8, [None] * 8]
            osbB_tok = [[None] * 8, [None] * 8]
            if True:
                def qk_both(p, w, qcols):
                    if w == 0:
                        jcs = [(0, 0, "oc0", 1), (0, 1, "oc1", 3)]
                    else:
                        jcs = [(w - 1, 0, "jc0", 0), (w - 1, 1, "jc1", 1),
                               (w, 0, "jc2", 2), (w, 1, "jc3", 3)]
                    ptiles = [[], []]
                    for idx, (kw, kcs, role, tno) in enumerate(jcs):
                        for hb in (0, 1):
                            qk_role(p, hb, hb * 64, w, qcols, idx, kw, kcs,
                                    role, tno, ptiles[hb])
                    return ptiles

                def qk_role(p, hb, hbase, w, qcols, idx, kw, kcs, role, tno, ptile):
                    if True:
                        pt_i = hb * 4 + tno
                        krsl = KR[hbase:hbase + 64,
                                  p * 768 + (kw % 3) * 256 + kcs * 128:
                                  p * 768 + (kw % 3) * 256 + kcs * 128 + 128]
                        bk = hb * 2 + (idx % 2)
                        B.acquire(bk)
                        S.wait("tensor", kr_wr[kw % 3])
                        if role == "jc0":
                            rhs = QB[hbase:hbase + 64, p * 512 + qcols: p * 512 + qcols + 128]
                            mm = S.op("tensor", lambda e, k=krsl, r=rhs, bk=bk: e.matmul(
                                ps[bk][:, 0:128], k, r, start=True, stop=True), inc=("t", 1))
                            kr_rd[kw % 3] = mm
                            S.wait("scalar", mm)
                            S.wait("scalar", pt_rd[pt_i])
                            ev = S.op("scalar", lambda e, bk=bk, pi=pt_i: e.activation(
                                out=PT[pi][:, 0:128], in_=ps[bk][:, 0:128],
                                func=AF.Exp, scale=8.0), inc=("s", 1))
                            S.wait("vector", ev)
                            mk = S.op("vector", lambda e, pi=pt_i: e.tensor_tensor(
                                out=PT[pi][:, 0:128], in0=PT[pi][:, 0:128],
                                in1=TGE, op=ALU.mult), inc=("v", 1))
                            pt_wr[pt_i] = mk
                        elif role in ("jc3", "oc1"):
                            rhs = QB[hbase:hbase + 64, p * 512 + qcols + 128: p * 512 + qcols + 256]
                            mm = S.op("tensor", lambda e, k=krsl, r=rhs, bk=bk: e.matmul(
                                ps[bk][:, 0:128], k, r, start=True, stop=True), inc=("t", 1))
                            kr_rd[kw % 3] = mm
                            S.wait("scalar", mm)
                            S.wait("scalar", pt_rd[pt_i])
                            ev = S.op("scalar", lambda e, bk=bk, pi=pt_i: e.activation(
                                out=PT[pi][:, 128:256], in_=ps[bk][:, 0:128],
                                func=AF.Exp, scale=8.0), inc=("s", 1))
                            S.wait("vector", ev)
                            mk = S.op("vector", lambda e, pi=pt_i: e.tensor_tensor(
                                out=PT[pi][:, 128:256], in0=PT[pi][:, 128:256],
                                in1=TLE, op=ALU.mult), inc=("v", 1))
                            pt_wr[pt_i] = mk
                        else:
                            rhs = QB[hbase:hbase + 64, p * 512 + qcols: p * 512 + qcols + 256]
                            mm = S.op("tensor", lambda e, k=krsl, r=rhs, bk=bk: e.matmul(
                                ps[bk][:, 0:256], k, r, start=True, stop=True), inc=("t", 1))
                            kr_rd[kw % 3] = mm
                            S.wait("scalar", mm)
                            S.wait("scalar", pt_rd[pt_i])
                            ev = S.op("scalar", lambda e, bk=bk, pi=pt_i: e.activation(
                                out=PT[pi][:, 0:256], in_=ps[bk][:, 0:256],
                                func=AF.Exp, scale=8.0), inc=("s", 1))
                            pt_wr[pt_i] = ev
                            S.wait("vector", ev)
                            if role == "jc1":
                                mk = S.op("vector", lambda e, pi=pt_i: e.tensor_tensor(
                                    out=PT[pi][:, 128:256], in0=PT[pi][:, 128:256],
                                    in1=TGE, op=ALU.mult), inc=("v", 1))
                            else:
                                mk = S.op("vector", lambda e, pi=pt_i: e.tensor_tensor(
                                    out=PT[pi][:, 0:128], in0=PT[pi][:, 0:128],
                                    in1=TLE, op=ALU.mult), inc=("v", 1))
                            pt_wr[pt_i] = mk
                        B.release(bk, pt_wr[pt_i])
                        ptile.append((pt_i, kw, kcs))

                def pv_accum(p, hb, head, ptile, ob):
                    nonlocal su_rd
                    B.acquire(ob)
                    nmm = len(ptile)
                    pv = None
                    for idx2, (pi, kw, kcs) in enumerate(ptile):
                        slot = (kw * 2 + kcs) % 6
                        vsl = VT[slot][:, head * 65: head * 65 + 65]
                        S.wait("tensor", pt_wr[pi])
                        S.wait("tensor", vt_wr[slot])
                        pv = S.op("tensor", lambda e, v=vsl, pi=pi, ob=ob,
                                  st=(idx2 == 0), sp=(idx2 == nmm - 1): e.matmul(
                            ps[ob][0:65, 0:256], v, PT[pi][:, :], start=st, stop=sp), inc=("t", 1))
                        pt_rd[pi] = pv
                        vt_rd[slot] = pv
                    S.wait("scalar", pv)
                    S.wait("scalar", su_rd)
                    S.op("scalar", lambda e, ob=ob, hb=hb: e.activation(
                        out=SU[64:65, hb * 256:hb * 256 + 256],
                        in_=ps[ob][64:65, 0:256], func=AF.Ln), inc=("s", 1))

                def pv_exp():
                    return S.op("scalar", lambda e: e.activation(
                        out=SU[64:65, :], in_=SU[64:65, :], func=AF.Exp, scale=-1.0), inc=("s", 1))

                def pv_finish(p, wl, qcols, pva, pvb, bb, l2):
                    nonlocal su_rd, rb_rd, osb_rd
                    B.acquire(bb)
                    S.wait("tensor", l2)
                    bc = S.op("tensor", lambda e, bb=bb: e.matmul(
                        ps[bb][0:64, 0:512], ONES[64:65, 0:64], SU[64:65, :],
                        start=True, stop=True), inc=("t", 1))
                    su_rd = bc
                    S.wait("scalar", bc)
                    S.wait("scalar", rb_rd)
                    rb = S.op("scalar", lambda e, bb=bb: e.activation(
                        out=RB[0:64, :], in_=ps[bb][0:64, 0:512], func=AF.Identity), inc=("s", 1))
                    B.release(bb, rb)
                    S.wait("vector", rb)
                    dst = OSB[0:64, p * 512 + qcols: p * 512 + qcols + 256]
                    S.wait("vector", osb_rd)
                    oa = S.op("vector", lambda e, d=dst, pva=pva: e.tensor_tensor(
                        out=d, in0=ps[pva][0:64, 0:256], in1=RB[0:64, 0:256],
                        op=ALU.mult), inc=("v", 1))
                    osbA_tok[wl][p] = oa
                    oi = p % 2
                    S.wait("vector", ott_rd[oi])
                    oe = S.op("vector", lambda e, oi=oi, pvb=pvb: e.tensor_tensor(
                        out=OTT[oi][0:64, :], in0=ps[pvb][0:64, 0:256],
                        in1=RB[0:64, 256:512], op=ALU.mult), inc=("v", 1))
                    rb_rd = oe
                    B.release(pva, oe)
                    B.release(pvb, oe)
                    S.wait("gpsimd", oe)
                    S.wait("gpsimd", osb_rd)
                    dtk = S.op("gpsimd", lambda e, oi=oi, p=p, q=qcols: e.dma_start(
                        out=OSB[64:128, p * 512 + q: p * 512 + q + 256],
                        in_=OTT[oi][0:64, :]), inc=(f"go{oi}", 16))
                    ott_rd[oi] = dtk
                    osbB_tok[wl][p] = dtk

            for wl in range(2):
                w = 2 * tau + wl
                qcols = wl * 256
                for p in range(8):
                    ptA, ptB = qk_both(p, w, qcols)
                    if pending is not None:
                        pv_finish(*pending)
                    pva, pvb = (4, 5) if p % 2 == 0 else (6, 7)
                    pv_accum(p, 0, 2 * p, ptA, pva)
                    pv_accum(p, 1, 2 * p + 1, ptB, pvb)
                    pending = (p, wl, qcols, pva, pvb,
                               0 if p % 2 == 0 else 2, pv_exp())
            pv_finish(*pending)
            pending = None
            wo_last = None
            if tau == 0 and prol_done[li] is not None:
                for tk_ in prol_done[li]:
                    S.wait("sync", tk_)
            for oc in range(8):
                oi = oc % 2
                S.wait("sync", oring_free[oi])
                otk = S.op("sync", lambda e, oc=oc, oi=oi, li=li: e.dma_start(
                    out=oring[oi][:, :].rearrange("p (c n) -> p c n", c=8),
                    in_=H["wo"][li, oc].rearrange("c p n -> p c n")), inc=(f"d{7 + oi}", 16))
                oring_rdy[oi] = otk
                S.wait("tensor", otk)
                B.acquire(oc)
                tk = None
                for ic in range(8):
                    if oc == 0:
                        for wl_ in range(2):
                            S.wait("tensor", osbA_tok[wl_][ic])
                            S.wait("tensor", osbB_tok[wl_][ic])
                    tk = S.op("tensor", lambda e, oi=oi, ic=ic, oc=oc, st=(ic == 0), sp=(ic == 7): e.matmul(
                        ps[oc][:, :], oring[oi][:, ic * 128:(ic + 1) * 128],
                        OSB[:, ic * 512:(ic + 1) * 512], start=st, stop=sp), inc=("t", 1))
                oring_free[oi] = tk
                wo_last = tk
                sl = ysl(oc, t0, TAU, False)
                S.wait("vector", tk)
                S.op("vector", lambda e, oc=oc, sl=sl: e.tensor_tensor(
                    out=sl[:, 0:511], in0=ps[oc][:, 0:511], in1=sl[:, 0:511],
                    op=ALU.add), inc=("v", 1))
                u2 = S.op("vector", lambda e, oc=oc, sl=sl, tau=tau: e.tensor_tensor(
                    out=DL[tau % 2][oc], in0=ps[oc][:, 511:512], in1=sl[:, 511:512],
                    op=ALU.add), inc=("v", 1))
                B.release(oc, u2)
            qb_rd = wo_last
            osb_rd = wo_last
            if TRUNC_AT == (li, tau, "wo"):
                return

        for oc in range(8):
            S.op("vector", lambda e, oc=oc: e.tensor_copy(
                out=Y[:, oc * Z + Z - 1: oc * Z + Z], in_=DL[(NT - 1) % 2][oc]), inc=("v", 1))
        S.barrier()

    def ffn_layer(li):
        ar = Area()
        XNF = [ar.take(512) for _ in range(8)]
        XNH = [ar.takeb(512) for _ in range(8)]
        XNL = [ar.takeb(512) for _ in range(8)]
        H2H = [ar.takeb(512) for _ in range(NHC)]
        H2L = [ar.takeb(512) for _ in range(NHC)]
        H2T = [ar.take(512), ar.take(512)]
        GSB = [ar.take(512), ar.take(512)]
        DLF = [[ar.take(1) for _ in range(8)] for _ in range(2)]
        w1ring = [ar.takeb(4096), ar.takeb(4096)]
        w2ring = [ar.takeb(2048), ar.takeb(2048)]
        lctx = LNCtx(ar)
        prol_off = PROL_BASE
        assert ar.off <= PROL_BASE
        w1_free = [None, None]
        w1_rdy = [None, None]
        w2_free = [None, None]
        w2_rdy = [None, None]
        xnh_rd = [None] * 8
        xnf_wr = [None] * 8
        h2_rd = [None] * NHC
        gsb_rd = [None, None]
        h2t_rd = [None, None]

        ltok = S.op("gpsimd", lambda e, li=li: e.dma_start(out=LPAR[:, :], in_=H["lpar"][li]), inc=("gl", 16))
        S.wait("vector", ltok)
        S.wait("scalar", ltok)
        if prol_done[li] is not None:
            for tk_ in prol_done[li]:
                S.wait("sync", tk_)

        for tau in range(NT):
            t0 = tau * TAU
            if li + 1 < N_LAYERS:
                emit_split_ffn(li + 1, prol_off, quarter=tau)
            tmu, trs = ln_compute(lctx, t0, True, 0, 1)
            last = None
            for d_ in range(8):
                sl = ysl(d_, t0, TAU, True)
                S.wait("vector", xnh_rd[d_])
                S.wait("vector", tmu)
                S.op("vector", lambda e, sl=sl, d=d_: e.scalar_tensor_tensor(
                    out=XNF[d], in0=sl, scalar=1.0, in1=lctx.MU,
                    op0=ALU.mult, op1=ALU.subtract), inc=("v", 1))
                S.wait("vector", trs)
                S.op("vector", lambda e, d=d_: e.scalar_tensor_tensor(
                    out=XNF[d], in0=XNF[d], scalar=LPAR[:, d:d + 1], in1=lctx.RS,
                    op0=ALU.mult, op1=ALU.mult), inc=("v", 1))
                ch = S.op("vector", lambda e, d=d_: e.tensor_scalar(
                    out=XNF[d], in0=XNF[d], scalar1=LPAR[:, 8 + d:8 + d + 1],
                    scalar2=None, op0=ALU.add), inc=("v", 1))
                S.wait("scalar", ch)
                S.wait("scalar", xnh_rd[d_])
                ct = S.op("scalar", lambda e, d=d_: e.activation(
                    out=XNH[d], in_=XNF[d], func=AF.Identity), inc=("s", 1))
                S.wait("vector", ct)
                sb = S.op("vector", lambda e, d=d_: e.tensor_tensor(
                    out=XNL[d], in0=XNF[d], in1=XNH[d], op=ALU.subtract), inc=("v", 1))
                xnf_wr[d_] = sb
                last = sb
            lctx.mu_rd = last
            lctx.rs_rd = last

            if tau > 0:
                for oc in range(8):
                    S.op("vector", lambda e, oc=oc, tp=t0 - 1, tau=tau: e.tensor_copy(
                        out=Y[:, oc * Z + 1 + tp: oc * Z + 2 + tp],
                        in_=DLF[(tau - 1) % 2][oc]), inc=("v", 1))

            for hc in range(NHC):
                wi = hc % 2
                S.wait("sync", w1_free[wi])
                wtk = S.op("sync", lambda e, hc=hc, wi=wi, li=li: e.dma_start(
                    out=w1ring[wi][:, :].rearrange("p (d n) -> p d n", d=8),
                    in_=H["s_w1"][li, hc].rearrange("d p n -> p d n")), inc=(f"d{4 + wi}", 16))
                w1_rdy[wi] = wtk
                S.wait("tensor", wtk)
                ba, bg = (hc % 2) * 2, (hc % 2) * 2 + 1
                B.acquire(ba)
                B.acquire(bg)
                tk = None
                atk = None
                slab = w1ring[wi][:, :].rearrange("p (d n) -> p d n", d=8)
                for d_ in range(8):
                    st = (d_ == 0)
                    if hc == 0:
                        S.wait("tensor", xnf_wr[d_])
                    for pi_, (wof, xx) in enumerate(((0, XNH[d_]), (0, XNL[d_]), (256, XNH[d_]))):
                        atk = S.op("tensor", lambda e, sl=slab, d=d_, w=wof, x=xx, ba=ba,
                                   st=(st and pi_ == 0), sp=(d_ == 7 and pi_ == 2): e.matmul(
                            ps[ba][:, :], sl[:, d, w:w + 128], x,
                            start=st, stop=sp), inc=("t", 1))
                        tk = S.op("tensor", lambda e, sl=slab, d=d_, w=wof + 128, x=xx, bg=bg,
                                  st=(st and pi_ == 0), sp=(d_ == 7 and pi_ == 2): e.matmul(
                            ps[bg][:, :], sl[:, d, w:w + 128], x,
                            start=st, stop=sp), inc=("t", 1))
                w1_free[wi] = tk
                for d_ in range(8):
                    xnh_rd[d_] = tk
                hp = hc % 2
                S.wait("scalar", atk)
                S.wait("scalar", tk)
                S.wait("scalar", h2t_rd[hp])
                ea = S.op("scalar", lambda e, ba=ba, hc=hc, hp=hp: e.activation(
                    out=H2T[hp], in_=ps[ba][:, :], func=AF.Identity,
                    bias=LPAR[:, 26 + hc:26 + hc + 1]), inc=("s", 1))
                B.release(ba, ea)
                S.wait("scalar", gsb_rd[hp])
                eg = S.op("scalar", lambda e, bg=bg, hc=hc, hp=hp: e.activation(
                    out=GSB[hp], in_=ps[bg][:, :], func=AF.Gelu,
                    bias=LPAR[:, 48 + hc:48 + hc + 1]), inc=("s", 1))
                B.release(bg, eg)
                S.wait("vector", eg)
                S.wait("vector", ea)
                h2m = S.op("vector", lambda e, hp=hp: e.tensor_tensor(
                    out=H2T[hp], in0=H2T[hp], in1=GSB[hp], op=ALU.mult), inc=("v", 1))
                gsb_rd[hp] = h2m
                S.wait("scalar", h2m)
                S.wait("scalar", h2_rd[hc])
                ch = S.op("scalar", lambda e, hc=hc, hp=hp: e.activation(
                    out=H2H[hc], in_=H2T[hp], func=AF.Identity), inc=("s", 1))
                S.wait("vector", ch)
                S.wait("vector", h2_rd[hc])
                sb = S.op("vector", lambda e, hc=hc, hp=hp: e.tensor_tensor(
                    out=H2L[hc], in0=H2T[hp], in1=H2H[hc], op=ALU.subtract), inc=("v", 1))
                h2t_rd[hp] = sb
                h2_rd[hc] = sb

            for hc in range(NHC):
                wi = hc % 2
                S.wait("sync", w2_free[wi])
                wtk = S.op("sync", lambda e, hc=hc, wi=wi, li=li: e.dma_start(
                    out=w2ring[wi][:, :], in_=H["s_w2"][li, hc]), inc=(f"d{6 + wi}", 16))
                w2_rdy[wi] = wtk
                S.wait("tensor", wtk)
                S.wait("tensor", h2_rd[hc])
                tk = None
                for oc in range(8):
                    if hc == 0:
                        B.acquire(oc)
                    for pi_, (wof, xx) in enumerate(((0, H2H[hc]), (0, H2L[hc]), (1024, H2H[hc]))):
                        tk = S.op("tensor", lambda e, wi=wi, oc=oc, hc=hc, w=wof, x=xx,
                                  st=(hc == 0 and pi_ == 0), sp=(hc == NHC - 1 and pi_ == 2): e.matmul(
                            ps[oc][:, :], w2ring[wi][:, w + oc * 128:w + (oc + 1) * 128], x,
                            start=st, stop=sp), inc=("t", 1))
                w2_free[wi] = tk
                h2_rd[hc] = tk
            w2_done = S.tok("t")
            for oc in range(8):
                sl = ysl(oc, t0, TAU, False)
                S.wait("vector", w2_done)
                S.op("vector", lambda e, oc=oc, sl=sl: e.scalar_tensor_tensor(
                    out=sl[:, 0:511], in0=ps[oc][:, 0:511],
                    scalar=LPAR[:, 16 + oc:16 + oc + 1], in1=sl[:, 0:511],
                    op0=ALU.add, op1=ALU.add), inc=("v", 1))
                u2 = S.op("vector", lambda e, oc=oc, sl=sl, tau=tau: e.scalar_tensor_tensor(
                    out=DLF[tau % 2][oc], in0=ps[oc][:, 511:512],
                    scalar=LPAR[:, 16 + oc:16 + oc + 1], in1=sl[:, 511:512],
                    op0=ALU.add, op1=ALU.add), inc=("v", 1))
                B.release(oc, u2)

        for oc in range(8):
            S.op("vector", lambda e, oc=oc: e.tensor_copy(
                out=Y[:, oc * Z + Z - 1: oc * Z + Z], in_=DLF[(NT - 1) % 2][oc]), inc=("v", 1))
        S.barrier()

    def final():
        ar = Area()
        XT = [ar.take(2048), ar.take(2048)]
        xt_rd = [None, None]
        for d_ in range(8):
            xi = d_ % 2
            S.wait("gpsimd", xt_rd[xi])
            ld = S.op("gpsimd", lambda e, d=d_, xi=xi: e.dma_start(
                out=XT[xi], in_=H["x1scr"][d * 128:(d + 1) * 128, :]), inc=(f"gf{xi}", 16))
            S.wait("vector", ld)
            ad = S.op("vector", lambda e, d=d_, xi=xi: e.tensor_tensor(
                out=XT[xi], in0=XT[xi], in1=yt(d)[:, 1:], op=ALU.add), inc=("v", 1))
            S.wait("gpsimd", ad)
            st = S.op("gpsimd", lambda e, d=d_, xi=xi: e.dma_start(
                out=H["out"][d * 128:(d + 1) * 128, :], in_=XT[xi]), inc=(f"gf{xi}", 16))
            xt_rd[xi] = st
        f0, f1 = S.tok("gf0"), S.tok("gf1")

        def fin(e, f0=f0):
            e.wait_ge(S.sems["gf0"], f0[1])
        S.q["gpsimd"].append(fin)

        def fin2(e, f1=f1):
            e.wait_ge(S.sems["gf1"], f1[1])
        S.q["gpsimd"].append(fin2)

    phase0()
    mamba()
    for li in range(N_LAYERS):
        attn_layer(li)
        if TRUNC or TRUNC_AT is not None:
            break
        ffn_layer(li)
    if not (TRUNC or TRUNC_AT is not None):
        final()

    sem_names = ["t", "v", "s", "g", "gd", "gx", "gl", "go0", "go1", "gf0", "gf1", "dt", "pi0", "pi1", "po0", "po1", "po2", "po3", "pg"] + [f"d{i}" for i in range(10)]
    with contextlib.ExitStack() as semstack:
        sems = {n: semstack.enter_context(nc.semaphore(f"sem_{n}")) for n in sem_names}
        blk = semstack.enter_context(nc.Block())
        block = blk
        S.sems = sems

        @block.tensor
        def _(eng):
            for fn in S.q["tensor"]:
                fn(eng)

        @block.vector
        def _(eng):
            for fn in S.q["vector"]:
                fn(eng)

        @block.scalar
        def _(eng):
            for fn in S.q["scalar"]:
                fn(eng)

        @block.sync
        def _(eng):
            for fn in S.q["sync"]:
                fn(eng)

        @block.gpsimd
        def _(eng):
            for fn in S.q["gpsimd"]:
                fn(eng)

    stack.close()
    return nc


def kernel(**inputs):
    pr = _prep(inputs)
    nc = build_nc()
    shared = {k: pr[k] for k in pr if k != "x0z"}
    in_maps = []
    for c in range(NCORES):
        m = dict(shared)
        m["x0z"] = np.ascontiguousarray(pr["x0z"][c])
        in_maps.append(m)
    res = run_bass_kernel_spmd(nc, in_maps, core_ids=list(range(NCORES)))
    out = np.stack([np.ascontiguousarray(np.asarray(res.results[c]["out"]).T)
                    for c in range(NCORES)])
    return out.astype(np.float32)
